# version 09
# baseline (speedup 1.0000x reference)
import numpy as np
import ml_dtypes

import concourse.bass as bass
import concourse.mybir as mybir
from concourse.bass_utils import run_bass_kernel_spmd

F32 = mybir.dt.float32
BF16 = mybir.dt.bfloat16
I32 = mybir.dt.int32
AF = mybir.ActivationFunctionType
ALU = mybir.AluOpType

P = 128
D = 2048
S = 4096
B = 4
H = 16
TOK = 2048
NQB = 16
NIC = D // P
INV_SQRT_DK = float(1.0 / np.sqrt(128.0))
EPS = 1e-5

ENGINES = ("pe", "act", "dve", "pool", "sp")


def stripe_list(parity):
    if parity == 0:
        return [g for g in range(2 * NQB) if g % 4 in (0, 3)]
    return [g for g in range(2 * NQB) if g % 4 in (1, 2)]


class Sched:

    def __init__(self):
        self.streams = {e: [] for e in ENGINES}
        self.cnt = {}

    def emit(self, eng, fn, wait=(), inc=None, inc_by=1):
        tok = None
        if inc is not None:
            self.cnt[inc] = self.cnt.get(inc, 0) + inc_by
            tok = self.cnt[inc]
        w = tuple((s, v) for (s, v) in wait if v)
        self.streams[eng].append((fn, w, inc, inc_by))
        return tok

    def tok(self, sem):
        return self.cnt.get(sem, 0)


def build_program(debug_taps=False):
    nc = bass.Bass("TRN2", target_bir_lowering=False)

    xTkv = nc.dram_tensor("xTkv", [D, S], F32, kind="ExternalInput").ap()
    xqT = nc.dram_tensor("xqT", [D, TOK], F32, kind="ExternalInput").ap()
    xresq = nc.dram_tensor("xresq", [TOK, D], F32, kind="ExternalInput").ap()
    wqT = nc.dram_tensor("wqT", [D, D], F32, kind="ExternalInput").ap()
    wkT = nc.dram_tensor("wkT", [D, D], F32, kind="ExternalInput").ap()
    wvT = nc.dram_tensor("wvT", [D, D], F32, kind="ExternalInput").ap()
    woT = nc.dram_tensor("woT", [D, D], F32, kind="ExternalInput").ap()
    pw = nc.dram_tensor("pw", [D, 4 * D], F32, kind="ExternalInput").ap()
    nw = nc.dram_tensor("nw", [D, 4 * D], F32, kind="ExternalInput").ap()
    pbias = nc.dram_tensor("pbias", [D, 4], F32, kind="ExternalInput").ap()
    nbias = nc.dram_tensor("nbias", [D, 4], F32, kind="ExternalInput").ap()
    vecs = nc.dram_tensor("vecs", [1, 16], F32, kind="ExternalInput").ap()
    dmask = nc.dram_tensor("dmask", [NQB * 2 * P, P], BF16, kind="ExternalInput").ap()

    y = nc.dram_tensor("y", [TOK, D], F32, kind="ExternalOutput").ap()

    KIND = "ExternalOutput" if debug_taps else "Internal"
    kdr = nc.dram_tensor("kdr", [D, S], BF16, kind=KIND).ap()
    vdr = nc.dram_tensor("vdr", [H * S, P], BF16, kind=KIND).ap()
    qdr = nc.dram_tensor("qdr", [D, TOK], BF16, kind=KIND).ap()
    adr = nc.dram_tensor("adr", [TOK, D], BF16, kind=KIND).ap()
    wtdr = nc.dram_tensor("wtdr", [D, D], BF16, kind=KIND).ap()
    y1dr = nc.dram_tensor("y1dr", [TOK, D], F32, kind=KIND).ap()
    bdr = nc.dram_tensor("bdr", [1, D], BF16, kind=KIND).ap()

    ident_c = nc.inline_tensor(np.eye(P, dtype=ml_dtypes.bfloat16), name="identc").ap()
    onesc_c = nc.inline_tensor(np.ones((P, 1), dtype=ml_dtypes.bfloat16), name="onescc").ap()
    onesr_c = nc.inline_tensor(np.ones((1, P), dtype=np.float32), name="onesrc").ap()
    onesrb_c = nc.inline_tensor(np.ones((1, P), dtype=ml_dtypes.bfloat16), name="onesrbc").ap()

    wa = nc.alloc_sbuf_tensor("wa", [P, NIC * D], BF16)
    pbf = nc.alloc_sbuf_tensor("pbf", [P, 16384], BF16)
    pf = nc.alloc_sbuf_tensor("pf", [P, 10240], F32)
    wstg = nc.alloc_sbuf_tensor("wstg", [P, 4096], BF16)
    sh2k = nc.alloc_sbuf_tensor("sh2k", [P, 2048], F32)
    stg = nc.alloc_sbuf_tensor("stg", [P, 4096], BF16)
    msk = nc.alloc_sbuf_tensor("msk", [P, NQB * 2 * P], BF16)
    idn = nc.alloc_sbuf_tensor("idn", [P, P], BF16)
    ones_col = nc.alloc_sbuf_tensor("ones_col", [P, 1], BF16)
    ones_row = nc.alloc_sbuf_tensor("ones_row", [1, P], F32)
    ones_row_bf = nc.alloc_sbuf_tensor("ones_row_bf", [1, P], BF16)
    vrow = nc.alloc_sbuf_tensor("vrow", [1, 32], F32)
    brow = nc.alloc_sbuf_tensor("brow", [1, D], BF16)
    lsum = nc.alloc_sbuf_tensor("lsum", [P, 1024], F32)
    lsd = nc.alloc_sbuf_tensor("lsd", [P, 4 * 512], F32)
    lsp = nc.alloc_sbuf_tensor("lsp", [P, 4 * 512], F32)
    lsbf = nc.alloc_sbuf_tensor("lsbf", [P, 1024], BF16)
    linv = nc.alloc_sbuf_tensor("linv", [P, 8], F32)
    aev = nc.alloc_sbuf_tensor("aev", [P, 1024], BF16)
    astg = nc.alloc_sbuf_tensor("astg", [P, 1024], BF16)
    zb = nc.alloc_sbuf_tensor("zb", [P, 1], F32)
    pbv = nc.alloc_sbuf_tensor("pbv", [P, 128], F32)
    bcol = nc.alloc_sbuf_tensor("bcol", [P, 16], F32)
    bcol_b = nc.alloc_sbuf_tensor("bcol_b", [P, 16], BF16)
    cbc = nc.alloc_sbuf_tensor("cbc", [P, 16], F32)
    smf = nc.alloc_sbuf_tensor("smf", [P, 16], F32)

    ps_s = nc.alloc_psum_tensor("ps_s", [P, 2048], F32)
    ps_a = nc.alloc_psum_tensor("ps_a", [P, 1024], F32)
    ps_t = nc.alloc_psum_tensor("ps_t", [P, 1024], BF16)
    ps_l = nc.alloc_psum_tensor("ps_l", [P, 512], F32)

    sch = Sched()
    E = sch.emit
    T = sch.tok

    E("pool", lambda g: g.dma_start(out=idn[:], in_=ident_c[:, :]), inc="qc0", inc_by=16)
    E("pool", lambda g: g.dma_start(out=ones_col[:], in_=onesc_c[:, :]), inc="qc0", inc_by=16)
    E("pool", lambda g: g.dma_start(out=ones_row[:], in_=onesr_c[:, :]), inc="qc0", inc_by=16)
    E("pool", lambda g: g.dma_start(out=ones_row_bf[:], in_=onesrb_c[:, :]), inc="qc0", inc_by=16)
    E("pool", lambda g: g.dma_start(out=vrow[:, 0:16], in_=vecs[:, :]), inc="qc0", inc_by=16)
    E("pool", lambda g: g.dma_start(
        out=msk[:].rearrange("p (j r q) -> p j r q", j=NQB, r=2),
        in_=dmask.rearrange("(j r p) q -> p j r q", j=NQB, r=2)), inc="qc0", inc_by=16)
    E("pool", lambda g: g.dma_start(
        out=pbv[:, 0:64].rearrange("p (t n) -> p t n", t=NIC),
        in_=pbias.rearrange("(t p) n -> p t n", p=P),
        allow_slow_non_contiguous=True), inc="qc0", inc_by=16)
    E("pool", lambda g: g.dma_start(
        out=pbv[:, 64:128].rearrange("p (t n) -> p t n", t=NIC),
        in_=nbias.rearrange("(t p) n -> p t n", p=P),
        allow_slow_non_contiguous=True), inc="qc0", inc_by=16)
    t_p0 = T("qc0")
    E("dve", lambda v: v.memset(zb[:], 0.0), wait=[("qc0", t_p0)], inc="sdve")

    E("act", lambda s: s.activation(vrow[:1, 16:20], vrow[:1, 8:12], AF.Sigmoid),
      wait=[("qc0", t_p0)], inc="sact")
    t_sg = T("sact")
    E("dve", lambda v: v.tensor_mul(vrow[:1, 16:20], vrow[:1, 16:20], vrow[:1, 0:4]),
      wait=[("sact", t_sg)], inc="sdve")
    E("dve", lambda v: v.tensor_scalar(vrow[:1, 16:20], vrow[:1, 16:20], vrow[:1, 12:13], None,
                                       ALU.mult), wait=[("sdve", T("sdve"))], inc="sdve")
    E("dve", lambda v: v.tensor_scalar(vrow[:1, 20:24], vrow[:1, 16:20], -1.0, None, ALU.mult),
      wait=[("sdve", T("sdve"))], inc="sdve")
    E("dve", lambda v: v.tensor_copy(vrow[:1, 24:28], vrow[:1, 4:8]), inc="sdve")
    E("dve", lambda v: v.tensor_copy(vrow[:1, 28:29], vrow[:1, 13:14]), inc="sdve")
    t_vr = T("sdve")
    t_bco = E("pe", lambda t: t.matmul(ps_l[:, 16:32], ones_row[:, :], vrow[:1, 16:32],
                                       start=True, stop=True),
              wait=[("sdve", t_vr), ("qc0", t_p0)], inc="spe")
    E("dve", lambda v: v.tensor_copy(cbc[:], ps_l[:, 16:32]), wait=[("spe", t_bco)], inc="sdve")
    t_cbc = T("sdve")

    E("dve", lambda v: v.tensor_sub(pbv[:, 0:64], pbv[:, 0:64], pbv[:, 64:128]),
      wait=[("sdve", t_cbc)], inc="sdve")

    def _braw(n):
        def f(v, n=n):
            d3 = pbv[:, 0:64].rearrange("p (t n) -> p t n", t=NIC)[:, :, n]
            if n == 0:
                return v.tensor_scalar(bcol[:], d3, cbc[:, 8 + n:9 + n], None, ALU.mult)
            return v.scalar_tensor_tensor(bcol[:], d3, cbc[:, 8 + n:9 + n], bcol[:],
                                          ALU.mult, ALU.add)
        return f
    for n in range(4):
        E("dve", _braw(n), wait=[("sdve", T("sdve"))], inc="sdve")
    E("dve", lambda v: v.tensor_scalar(bcol[:], bcol[:], 1.0, -1.0, ALU.min, ALU.max),
      wait=[("sdve", T("sdve"))], inc="sdve")
    E("dve", lambda v: v.tensor_scalar(bcol[:], bcol[:], 15.0, None, ALU.mult),
      wait=[("sdve", T("sdve"))], inc="sdve")
    E("dve", lambda v: v.tensor_scalar(bcol[:], bcol[:], 12582912.0, -12582912.0,
                                       ALU.add, ALU.add), wait=[("sdve", T("sdve"))], inc="sdve")
    E("dve", lambda v: v.tensor_scalar(bcol[:], bcol[:], 1.0 / 15.0, None, ALU.mult),
      wait=[("sdve", T("sdve"))], inc="sdve")
    E("dve", lambda v: v.tensor_scalar(bcol_b[:], bcol[:], cbc[:, 12:13], None, ALU.mult),
      wait=[("sdve", T("sdve"))], inc="sdve")
    t_bc = T("sdve")
    t_btr = E("pe", lambda t: t.transpose(ps_t[:16, 0:P], bcol_b[:], idn[:]),
              wait=[("sdve", t_bc), ("qc0", t_p0)], inc="spe")
    E("dve", lambda v: v.tensor_copy(stg[:16, 0:P], ps_t[:16, 0:P]), wait=[("spe", t_btr)],
      inc="sdve")
    E("sp", lambda s: s.dma_start(out=bdr[:, :].rearrange("one (t p) -> (one t) p", t=16),
                                  in_=stg[:16, 0:P]),
      wait=[("sdve", T("sdve"))], inc="qsp", inc_by=16)
    E("sp", lambda s: s.dma_start(out=brow[:, :], in_=bdr[:, :]),
      wait=[("qsp", T("qsp"))], inc="qsp", inc_by=16)
    t_brow = T("qsp")

    xck = [pbf[:, 0:8192], pbf[:, 8192:16384]]

    def xc3(buf):
        return buf.rearrange("p (c t) -> p c t", c=NIC)

    def load_weight(dram_w, cast, wait_extra=()):
        for half in range(2):
            def f(g, half=half, dram_w=dram_w):
                return g.dma_start(
                    out=wa[:, half * 8 * D:(half + 1) * 8 * D].rearrange("p (c o) -> p c o", c=8),
                    in_=dram_w[half * 1024:(half + 1) * 1024, :].rearrange("(c p) o -> p c o", p=P))
            E("pool", f, wait=(list(wait_extra) if half == 0 else ()), inc="qw", inc_by=16)
        return T("qw")

    pslot = [ps_s[:, i * 512:(i + 1) * 512] for i in range(4)]
    slot_free = [None, None, None, None]
    xbuf_free = [None, None]

    pwb = [pf[:, i * 1024:(i + 1) * 1024] for i in range(4)]
    nwb = [pf[:, 4096 + i * 1024:4096 + (i + 1) * 1024] for i in range(4)]
    acc = pf[:, 8192:10240]
    sgsh = [sh2k[:, 0:1024], sh2k[:, 1024:2048]]

    wb_sched = [("load", 0)]
    for q in range(128):
        if q + 1 < 128:
            wb_sched.append(("load", q + 1))
        wb_sched.append(("comp", q))
        if q % 8 == 7:
            wb_sched.append(("fin", q // 8))
    wb_state = {
        "pw_free": [None] * 4, "nw_free": [None] * 4,
        "ld_p": [None] * 128, "ld_n": [None] * 128,
        "acc_tok": [None, None], "sgs_tok": [None, None],
        "wt_dma": None, "pst_tok": None, "idx": 0,
    }

    def wstep():
        st = wb_state
        if st["idx"] >= len(wb_sched):
            return
        kind, q = wb_sched[st["idx"]]
        st["idx"] += 1
        if kind == "load":
            sl, nh = q // 8, q % 8
            n, ih = nh // 2, nh % 2
            bb = q % 4
            col0 = n * D + ih * 1024
            qs = "qa%d" % (q % 2)
            w = [("sact", st["pw_free"][bb])] if st["pw_free"][bb] else []
            E("act", (lambda s, sl=sl, col0=col0, bb=bb:
                      s.dma_start(out=pwb[bb], in_=pw[sl * P:(sl + 1) * P, col0:col0 + 1024])),
              wait=w, inc=qs, inc_by=16)
            w = [("sact", st["nw_free"][bb])] if st["nw_free"][bb] else []
            E("act", (lambda s, sl=sl, col0=col0, bb=bb:
                      s.dma_start(out=nwb[bb], in_=nw[sl * P:(sl + 1) * P, col0:col0 + 1024])),
              wait=w, inc=qs, inc_by=16)
            st["ld_p"][q] = T(qs)
            st["ld_n"][q] = T(qs)
        elif kind == "comp":
            sl, nh = q // 8, q % 8
            n, ih = nh // 2, nh % 2
            bb = q % 4
            sgp = q % 2
            sgs = sgsh[sgp]
            accsl = acc[:, ih * 1024:(ih + 1) * 1024]
            w = [("qa%d" % (q % 2), st["ld_n"][q]), ("sdve", t_cbc)]
            if st["sgs_tok"][sgp]:
                w.append(("sdve", st["sgs_tok"][sgp]))
            E("act", (lambda s, bb=bb, sgs=sgs: s.activation(sgs, pwb[bb], AF.Sigmoid)),
              wait=w, inc="sact")
            st["pw_free"][bb] = T("sact")

            def stt_p(v, n=n, accsl=accsl, sgs=sgs):
                if n == 0:
                    return v.tensor_scalar(accsl, sgs, cbc[:, 0:1], None, ALU.mult)
                return v.scalar_tensor_tensor(accsl, sgs, cbc[:, n:n + 1], accsl, ALU.mult, ALU.add)
            w = [("sact", st["pw_free"][bb])]
            if st["acc_tok"][ih]:
                w.append(("sdve", st["acc_tok"][ih]))
            E("dve", stt_p, wait=w, inc="sdve")
            t_sttp = T("sdve")
            E("act", (lambda s, bb=bb, sgs=sgs: s.activation(sgs, nwb[bb], AF.Sigmoid)),
              wait=[("qa%d" % (q % 2), st["ld_n"][q]), ("sdve", t_sttp)], inc="sact")
            st["nw_free"][bb] = T("sact")

            def stt_n(v, n=n, accsl=accsl, sgs=sgs):
                return v.scalar_tensor_tensor(accsl, sgs, cbc[:, 4 + n:5 + n], accsl,
                                              ALU.mult, ALU.add)
            E("dve", stt_n, wait=[("sact", st["nw_free"][bb]), ("sdve", t_sttp)], inc="sdve")
            st["acc_tok"][ih] = T("sdve")
            st["sgs_tok"][sgp] = T("sdve")
        else:
            sl = q
            w = [("sdve", st["acc_tok"][0] or 0), ("sdve", st["acc_tok"][1] or 0)]
            if st["wt_dma"]:
                w.append(("qwt", st["wt_dma"]))
            E("dve", lambda v: v.tensor_copy(wstg[:, 0:2048], acc),
              wait=w + [("sdve", T("sdve"))], inc="sdve")
            t_wbf = T("sdve")
            for gr in range(2):
                w = [("sdve", t_wbf)]
                if st["pst_tok"]:
                    w.append(("sdve", st["pst_tok"]))
                for icb in range(8):
                    ic = gr * 8 + icb
                    def tr(t, ic=ic, icb=icb):
                        return t.transpose(ps_t[:, icb * P:(icb + 1) * P],
                                           wstg[:, ic * P:(ic + 1) * P], idn[:])
                    E("pe", tr, wait=(w if icb == 0 else ()), inc=("spe" if icb == 7 else None))
                t_tr = T("spe")
                E("dve", (lambda v, gr=gr:
                          v.tensor_copy(wstg[:, 2048 + gr * 1024:2048 + (gr + 1) * 1024], ps_t[:])),
                  wait=[("spe", t_tr)], inc="sdve")
                st["pst_tok"] = T("sdve")

            def wt_out(s, sl=sl):
                return s.dma_start(
                    out=wtdr.rearrange("(ic p) o -> p ic o", p=P)[:, :, sl * P:(sl + 1) * P],
                    in_=wstg[:, 2048:4096].rearrange("p (ic o) -> p ic o", ic=NIC))
            E("sp", wt_out, wait=[("sdve", st["pst_tok"])], inc="qwt", inc_by=16)
            st["wt_dma"] = T("qwt")

    def wdrain():
        while wb_state["idx"] < len(wb_sched):
            wstep()

    t_wk = load_weight(wkT, True)
    for tc in range(8):
        bb = tc % 2
        w = [("qw", t_wk)]
        if xbuf_free[bb] is not None:
            w.append(("spe", xbuf_free[bb]))

        def lx(g, tc=tc, bb=bb):
            return g.dma_start(out=xck[bb].rearrange("p (c t) -> p c t", c=NIC),
                               in_=xTkv[:, tc * 512:(tc + 1) * 512]
                               .rearrange("(c p) t -> p c t", p=P))
        E("pool", lx, wait=w, inc="qx%d" % bb, inc_by=16)
        t_x = T("qx%d" % bb)
        for oc in range(NIC):
            sl = (tc * NIC + oc) % 4
            stgsl = stg[:, sl * 512:(sl + 1) * 512]
            w = [("qx%d" % bb, t_x)]
            if slot_free[sl] is not None:
                w.append(("sdve", slot_free[sl]))
            for ic in range(NIC):
                def mm(t, ic=ic, oc=oc, bb=bb, sl=sl):
                    src = xck[bb].rearrange("p (c t) -> p c t", c=NIC)[:, ic, :]
                    return t.matmul(pslot[sl],
                                    wa[:, ic * D + oc * P:ic * D + (oc + 1) * P],
                                    src, start=(ic == 0), stop=(ic == NIC - 1))
                E("pe", mm, wait=(w if ic == 0 else ()), inc=("spe" if ic == NIC - 1 else None))
            t_mm = T("spe")
            if oc == NIC - 1:
                xbuf_free[bb] = t_mm
            E("dve", (lambda v, sl=sl, stgsl=stgsl: v.tensor_copy(stgsl, pslot[sl])),
              wait=[("spe", t_mm)], inc="sdve")
            slot_free[sl] = T("sdve")

            def kout(s, oc=oc, tc=tc, stgsl=stgsl):
                return s.dma_start(out=kdr[oc * P:(oc + 1) * P, tc * 512:(tc + 1) * 512], in_=stgsl)
            E("sp", kout, wait=[("sdve", slot_free[sl])], inc="qsp", inc_by=16)
            wstep()

    t_wv = load_weight(wvT, True, wait_extra=[("spe", T("spe"))])
    for tc in range(8):
        bb = tc % 2
        w = [("qw", t_wv)]
        if xbuf_free[bb] is not None:
            w.append(("spe", xbuf_free[bb]))

        def lx(g, tc=tc, bb=bb):
            return g.dma_start(out=xck[bb].rearrange("p (c t) -> p c t", c=NIC),
                               in_=xTkv[:, tc * 512:(tc + 1) * 512]
                               .rearrange("(c p) t -> p c t", p=P))
        E("pool", lx, wait=w, inc="qx%d" % bb, inc_by=16)
        t_x = T("qx%d" % bb)
        for tb in range(4):
            for op in range(4):
                sl = (tb * 4 + op) % 4
                stgsl = stg[:, sl * 512:(sl + 1) * 512]
                w = [("qx%d" % bb, t_x)]
                if slot_free[sl] is not None:
                    w.append(("sdve", slot_free[sl]))
                for ic in range(NIC):
                    def mm(t, ic=ic, tb=tb, op=op, bb=bb, sl=sl):
                        src = xck[bb].rearrange("p (c t) -> p c t", c=NIC)[:, ic, tb * P:(tb + 1) * P]
                        return t.matmul(pslot[sl], src,
                                        wa[:, ic * D + op * 512:ic * D + (op + 1) * 512],
                                        start=(ic == 0), stop=(ic == NIC - 1))
                    E("pe", mm, wait=(w if ic == 0 else ()), inc=("spe" if ic == NIC - 1 else None))
                t_mm = T("spe")
                if tb == 3 and op == 3:
                    xbuf_free[bb] = t_mm
                E("dve", (lambda v, sl=sl, stgsl=stgsl: v.tensor_copy(stgsl, pslot[sl])),
                  wait=[("spe", t_mm)], inc="sdve")
                slot_free[sl] = T("sdve")

                def vout(s, tc=tc, tb=tb, op=op, stgsl=stgsl):
                    tok0 = tc * 512 + tb * P
                    dst = vdr.rearrange("(h t) v -> h t v", h=H)
                    return s.dma_start(
                        out=dst[op * 4:(op + 1) * 4, tok0:tok0 + P, :].rearrange("h p v -> p h v"),
                        in_=stgsl.rearrange("p (h v) -> p h v", h=4))
                E("sp", vout, wait=[("sdve", slot_free[sl])], inc="qsp", inc_by=16)
                wstep()

    t_wq = load_weight(wqT, True, wait_extra=[("spe", T("spe"))])
    for qc in range(4):
        bb = qc % 2
        w = [("qw", t_wq)]
        if xbuf_free[bb] is not None:
            w.append(("spe", xbuf_free[bb]))

        def lx(g, qc=qc, bb=bb):
            return g.dma_start(out=xck[bb].rearrange("p (c t) -> p c t", c=NIC),
                               in_=xqT[:, qc * 512:(qc + 1) * 512]
                               .rearrange("(c p) t -> p c t", p=P))
        E("pool", lx, wait=w, inc="qx%d" % bb, inc_by=16)
        t_x = T("qx%d" % bb)
        for oc in range(NIC):
            sl = (qc * NIC + oc) % 4
            stgsl = stg[:, sl * 512:(sl + 1) * 512]
            w = [("qx%d" % bb, t_x)]
            if slot_free[sl] is not None:
                w.append(("sdve", slot_free[sl]))
            for ic in range(NIC):
                def mm(t, ic=ic, oc=oc, bb=bb, sl=sl):
                    src = xck[bb].rearrange("p (c t) -> p c t", c=NIC)[:, ic, :]
                    return t.matmul(pslot[sl],
                                    wa[:, ic * D + oc * P:ic * D + (oc + 1) * P],
                                    src, start=(ic == 0), stop=(ic == NIC - 1))
                E("pe", mm, wait=(w if ic == 0 else ()), inc=("spe" if ic == NIC - 1 else None))
            t_mm = T("spe")
            if oc == NIC - 1:
                xbuf_free[bb] = t_mm
            E("dve", (lambda v, sl=sl, stgsl=stgsl:
                      v.tensor_scalar(stgsl, pslot[sl], INV_SQRT_DK, None, ALU.mult)),
              wait=[("spe", t_mm)], inc="sdve")
            slot_free[sl] = T("sdve")

            def qout(s, oc=oc, qc=qc, stgsl=stgsl):
                return s.dma_start(out=qdr[oc * P:(oc + 1) * P, qc * 512:(qc + 1) * 512], in_=stgsl)
            E("sp", qout, wait=[("sdve", slot_free[sl])], inc="qsp", inc_by=16)
            wstep()
            wstep()
    wdrain()

    qb_ = [pbf[:, 0:2048], pbf[:, 2048:4096]]
    kb_ = [pbf[:, 4096:8192], pbf[:, 8192:12288]]
    vb_ = [wa[:, 0:4096], wa[:, 4096:8192]]
    ptb = [pbf[:, 12288 + i * 1024:12288 + (i + 1) * 1024] for i in range(4)]

    t_p1_pe = T("spe")
    t_p1_sp = T("qsp")
    t_p1_dve = T("sdve")

    q_used = [t_p1_pe, t_p1_pe]
    k_used = [t_p1_pe, t_p1_pe]
    v_used = [t_p1_pe, t_p1_pe]
    hl_tok = [None] * H

    def emit_head_loads(h):
        bb = h % 2
        qs = "qh%d" % bb
        w = [("qsp", t_p1_sp), ("spe", max(q_used[bb], k_used[bb], v_used[bb]))]
        E("sp", (lambda s, h=h, bb=bb: s.dma_start(out=qb_[bb], in_=qdr[h * P:(h + 1) * P, :])),
          wait=w, inc=qs, inc_by=16)
        E("sp", (lambda s, h=h, bb=bb: s.dma_start(out=kb_[bb], in_=kdr[h * P:(h + 1) * P, :])),
          inc=qs, inc_by=16)
        for q4 in range(4):
            def vf(s, h=h, bb=bb, q4=q4):
                return s.dma_start(
                    out=vb_[bb][:, q4 * 1024:(q4 + 1) * 1024].rearrange("p (r v) -> p r v", r=8),
                    in_=vdr[h * S + q4 * 1024:h * S + (q4 + 1) * 1024, :]
                        .rearrange("(r p) v -> p r v", p=P))
            E("sp", vf, inc=qs, inc_by=16)
        hl_tok[h] = T(qs)

    emit_head_loads(0)
    emit_head_loads(1)

    sb_s = [ps_s[:, 0:1024], ps_s[:, 1024:2048]]
    s_free = [None, None]
    pt_rd = [None, None, None, None]
    pt_rd_dve = [None, None, None, None]
    pt_rd_poolv = [None, None, None, None]
    a_free = [None, None]
    aev_free = [None, None]
    lsum_free = [None, None]
    lcomb_done = [None, None]
    astg_free = [None, None]
    lcol_rd = None
    atm_rd = None

    hsb = 0
    pending_fin = [None]

    def emit_finalize(fx):
        nonlocal lcol_rd, atm_rd
        ab = fx["ab"]; abank = fx["abank"]
        lsum_b = fx["lsum_b"]; lsbf_b = fx["lsbf_b"]; linv_b = fx["linv_b"]
        aev_b = fx["aev_b"]; astg_b = fx["astg_b"]; a_ps = fx["a_ps"]
        lA, lB = fx["lA"], fx["lB"]
        lC, lD = fx["lC"], fx["lD"]
        E("pool", (lambda g2, lC=lC, lD=lD: g2.tensor_tensor(lC, lC, lD, ALU.add)),
          wait=[("spool", fx["t_ls_pool"]), ("spool", T("spool"))], inc="spool")
        t_cd = T("spool")
        E("dve", (lambda v, lA=lA, lB=lB, lsum_b=lsum_b: v.tensor_add(lsum_b, lA, lB)),
          wait=[("sdve", fx["t_ls_done"]), ("sdve", T("sdve"))], inc="sdve")
        E("dve", (lambda v, lC=lC, lsum_b=lsum_b: v.tensor_add(lsum_b, lsum_b, lC)),
          wait=[("sdve", T("sdve")), ("spool", t_cd)], inc="sdve")
        lcomb_done[ab] = T("sdve")
        t_comb = T("sdve")
        E("act", (lambda s2, lsum_b=lsum_b, lsbf_b=lsbf_b:
                  s2.activation(lsbf_b, lsum_b, AF.Copy)),
          wait=[("sdve", t_comb)], inc="sact")
        t_lc = T("sact")
        w = [("sact", t_lc)]
        if lcol_rd is not None:
            w.append(("sdve", lcol_rd))
        for jj in range(4):
            def lm(t, jj=jj, lsbf_b=lsbf_b):
                return t.matmul(ps_l[:, jj:jj + 1], lsbf_b[:, jj * P:(jj + 1) * P],
                                ones_col[:], start=True, stop=True)
            E("pe", lm, wait=(w if jj == 0 else ()), inc=("spe" if jj == 3 else None))
        lsum_free[ab] = T("spe")
        E("dve", (lambda v, linv_b=linv_b: v.reciprocal(linv_b, ps_l[:, 0:4])),
          wait=[("spe", lsum_free[ab])], inc="sdve")
        lcol_rd = T("sdve")
        t_linv = lcol_rd

        w = [("spe", fx["t_pv_done"])]
        if aev_free[ab] is not None:
            w.append(("spe", aev_free[ab]))
        E("act", (lambda s2, a_ps=a_ps, aev_b=aev_b: s2.activation(aev_b, a_ps, AF.Copy)),
          wait=w, inc="sact")
        a_free[abank] = T("sact")
        t_aev = a_free[abank]

        w = [("sact", t_aev)]
        if atm_rd is not None:
            w.append(("sact", atm_rd))
        for jj in range(4):
            def tr(t, jj=jj, aev_b=aev_b):
                return t.transpose(ps_t[:, jj * P:(jj + 1) * P],
                                   aev_b[:, jj * P:(jj + 1) * P], idn[:])
            E("pe", tr, wait=(w if jj == 0 else ()), inc=("spe" if jj == 3 else None))
        aev_free[ab] = T("spe")
        t_atm = aev_free[ab]

        w = [("spe", t_atm), ("sdve", t_linv)]
        if astg_free[ab] is not None:
            w.append(("qad%d" % ab, astg_free[ab]))
        for jj in range(4):
            def dv(s2, jj=jj, linv_b=linv_b, astg_b=astg_b):
                return s2.activation(astg_b[:, jj * P:(jj + 1) * P],
                                     ps_t[:, jj * P:(jj + 1) * P],
                                     AF.Copy, bias=0.0, scale=linv_b[:, jj:jj + 1])
            E("act", dv, wait=(w if jj == 0 else ()), inc=("sact" if jj == 3 else None))
        atm_rd = T("sact")
        t_div = atm_rd

        def aout(g2, h=fx["h"], sb=fx["sb"], astg_b=astg_b):
            return g2.dma_start(
                out=adr.rearrange("(tb p) d -> p tb d", p=P)[:, 4 * sb:4 * sb + 4,
                                                             h * P:(h + 1) * P],
                in_=astg_b.rearrange("p (tb v) -> p tb v", tb=4))
        E("pool", aout, wait=[("sact", t_div)], inc="qad%d" % ab, inc_by=16)
        astg_free[ab] = T("qad%d" % ab)

    for h in range(H):
        bb = h % 2
        for sb in range(4):
            ab = hsb % 2
            abank = sb % 2
            npairs = 4 * (sb + 1)
            lsum_b = lsum[:, ab * 512:(ab + 1) * 512]
            lsbf_b = lsbf[:, ab * 512:(ab + 1) * 512]
            linv_b = linv[:, ab * 4:(ab + 1) * 4]
            aev_b = aev[:, ab * 512:(ab + 1) * 512]
            astg_b = astg[:, ab * 512:(ab + 1) * 512]
            a_ps = ps_a[:, abank * 512:(abank + 1) * 512]
            lA = lsd[:, (ab * 2 + 0) * 512:(ab * 2 + 1) * 512]
            lB = lsd[:, (ab * 2 + 1) * 512:(ab * 2 + 2) * 512]
            lC = lsp[:, (ab * 2 + 0) * 512:(ab * 2 + 1) * 512]
            lD = lsp[:, (ab * 2 + 1) * 512:(ab * 2 + 2) * 512]

            if sb == 0:
                wz = [("sdve", lcomb_done[ab] or 0), ("sdve", T("sdve"))]
                E("dve", (lambda v, lB=lB: v.memset(lB[:, 0:P], 0.0)), wait=wz, inc="sdve")
                wz = [("sdve", lcomb_done[ab] or 0), ("spool", T("spool"))]
                E("pool", (lambda g2, lD=lD: g2.memset(lD[:, 0:P], 0.0)), wait=wz, inc="spool")

            pending_pv = None

            def flush_pv():
                nonlocal pending_pv
                if pending_pv is None:
                    return
                ops, ptv_, is_last = pending_pv
                for r, (fn, w_) in enumerate(ops):
                    E("pe", fn, wait=(w_ if r == 0 else ()), inc=("spe" if r == 1 else None))
                pt_rd[ptv_] = T("spe")
                if is_last:
                    v_used[bb] = max(v_used[bb], pt_rd[ptv_])
                pending_pv = None

            for g in range(npairs):
                sbuf = g % 2
                off = max(0, g - 4 * sb)
                boundary = 4 * sb <= g <= 4 * sb + 3

                w = [("qh%d" % bb, hl_tok[h])]
                if s_free[sbuf] is not None:
                    w.append(("sact", s_free[sbuf]))
                elif hsb == 0:
                    w.append(("sdve", t_p1_dve))
                for r in range(2):
                    kb = 2 * g + r
                    def qk(t, r=r, kb=kb, off=off, sbuf=sbuf, bb=bb, sb=sb):
                        return t.matmul(sb_s[sbuf][:, r * 512 + off * P:(r + 1) * 512],
                                        kb_[bb][:, kb * P:(kb + 1) * P],
                                        qb_[bb][:, sb * 512 + off * P:(sb + 1) * 512],
                                        start=True, stop=True)
                    E("pe", qk, wait=(w if r == 0 else ()), inc=("spe" if r == 1 else None))
                t_sfull = T("spe")
                if g == npairs - 1:
                    k_used[bb] = max(k_used[bb], t_sfull)
                    if sb == 3:
                        q_used[bb] = max(q_used[bb], t_sfull)

                flush_pv()

                if g == 3 and pending_fin[0] is not None:
                    emit_finalize(pending_fin[0])
                    pending_fin[0] = None

                ptv = g % 4
                w = [("spe", t_sfull)]
                if pt_rd[ptv] is not None:
                    w.append(("spe", pt_rd[ptv]))
                if pt_rd_dve[ptv] is not None:
                    w.append(("sdve", pt_rd_dve[ptv]))
                if pt_rd_poolv[ptv] is not None:
                    w.append(("spool", pt_rd_poolv[ptv]))

                def ex(s, sbuf=sbuf, ptv=ptv, off=off):
                    sin = sb_s[sbuf].rearrange("p (r c) -> p r c", r=2)[:, :, off * P:]
                    pout = ptb[ptv].rearrange("p (r c) -> p r c", r=2)[:, :, off * P:]
                    return s.activation(pout, sin, AF.Exp, bias=zb[:], scale=1.0)
                E("act", ex, wait=w, inc="sact")
                s_free[sbuf] = T("sact")
                t_exp = s_free[sbuf]

                t_msk = None
                if boundary:
                    j = g
                    for r in range(2):
                        def mk(v, ptv=ptv, r=r, off=off, j=j):
                            slc = ptb[ptv][:, r * 512 + off * P:r * 512 + (off + 1) * P]
                            m3 = msk.rearrange("p (j r q) -> p j r q", j=NQB, r=2)[:, j, r, :]
                            return v.tensor_tensor(slc, slc, m3, ALU.mult)
                        E("dve", mk, wait=([("sact", t_exp), ("qc0", t_p0),
                                            ("sdve", T("sdve"))] if r == 0 else ()),
                          inc=("sdve" if r == 1 else None))
                    t_msk = T("sdve")

                w = [("sact", t_exp)]
                if t_msk is not None:
                    w.append(("sdve", t_msk))
                if g <= 1 and lsum_free[ab] is not None:
                    w.append(("spe", lsum_free[ab]))
                par = g % 2
                first = g <= 1
                dstd = (lA if par == 0 else lB)[:, off * P:]
                dstp = (lC if par == 0 else lD)[:, off * P:]
                def ls(v, ptv=ptv, off=off, dstd=dstd, first=first):
                    srcv = ptb[ptv][:, off * P:512]
                    if first:
                        return v.tensor_copy(dstd, srcv)
                    return v.scalar_tensor_tensor(dstd, srcv, 1.0, dstd, ALU.mult, ALU.add)
                E("dve", ls, wait=list(w) + [("sdve", T("sdve"))], inc="sdve")
                def lsq(gp, ptv=ptv, off=off, dstp=dstp, first=first):
                    srcv = ptb[ptv][:, 512 + off * P:1024]
                    if first:
                        return gp.tensor_copy(dstp, srcv)
                    return gp.tensor_tensor(dstp, srcv, dstp, ALU.add)
                wq = list(w) + [("spool", T("spool"))]
                if first and lcomb_done[ab] is not None:
                    wq.append(("sdve", lcomb_done[ab]))
                E("pool", lsq, wait=wq, inc="spool")
                pt_rd_dve[ptv] = T("sdve")
                pt_rd_poolv[ptv] = T("spool")

                w = [("sact", t_exp)]
                if t_msk is not None:
                    w.append(("sdve", t_msk))
                if g == 0 and a_free[abank] is not None:
                    w.append(("sact", a_free[abank]))
                ops = []
                for r in range(2):
                    kb = 2 * g + r
                    def pv(t, r=r, kb=kb, off=off, ptv=ptv, bb=bb, g=g, a_ps=a_ps, npairs=npairs):
                        return t.matmul(a_ps[:, off * P:],
                                        vb_[bb][:, kb * P:(kb + 1) * P],
                                        ptb[ptv][:, r * 512 + off * P:(r + 1) * 512],
                                        start=(g == 0 and r == 0),
                                        stop=(g == npairs - 1 and r == 1))
                    ops.append((pv, w))
                pending_pv = (ops, ptv, g == npairs - 1)
            flush_pv()
            pending_fin[0] = {
                "h": h, "sb": sb, "ab": ab, "abank": abank,
                "lsum_b": lsum_b, "lsbf_b": lsbf_b, "linv_b": linv_b,
                "aev_b": aev_b, "astg_b": astg_b, "a_ps": a_ps,
                "lA": lA, "lB": lB, "lC": lC, "lD": lD,
                "t_pv_done": T("spe"), "t_ls_done": T("sdve"), "t_ls_pool": T("spool"),
            }
            hsb += 1
        if h + 2 < H:
            emit_head_loads(h + 2)
    emit_finalize(pending_fin[0])
    pending_fin[0] = None

    t_p2_pe = T("spe")
    t_p2_ad0 = T("qad0")
    t_p2_ad1 = T("qad1")
    t_p2_dve = T("sdve")

    t_wo = load_weight(woT, True, wait_extra=[("spe", t_p2_pe)])

    atl = [pbf[:, 0:2048], pbf[:, 2048:4096]]
    afm2 = [pbf[:, 4096:6144], pbf[:, 6144:8192]]
    sqs = pbf[:, 8192:10240]
    y1bf = pbf[:, 10240:12288]
    y1fm2 = [pbf[:, 0:2048], pbf[:, 2048:4096]]
    xrl = [pf[:, 0:2048], pf[:, 2048:4096]]
    rr = [pf[:, 4096:6144], pf[:, 6144:8192]]

    atl_used = [t_p2_pe, t_p2_pe]
    xr_used = [None, None]
    afm_used = [None, None]
    pst_free = [None, None]
    pst_eng = ["sdve", "sdve"]
    alt_ps = ps_a[:, 0:512].bitcast(BF16)
    wo_ps_free = [None, None]
    r_free = [None, None]

    def emit_ln(r_t, outbuf, t_ready, extra_wait, phase_tag):
        E("act", (lambda s2, r_t=r_t: s2.activation(sqs, r_t, AF.Square, accum_out=smf[:, 3:4])),
          wait=[("sdve", t_ready)], inc="sact")
        t_sq = T("sact")
        E("dve", lambda v: v.tensor_add(smf[:, 2:3], smf[:, 0:1], smf[:, 1:2]),
          wait=[("sdve", t_ready)], inc="sdve")
        E("dve", lambda v: v.tensor_scalar(smf[:, 2:3], smf[:, 2:3], -1.0 / D, None, ALU.mult),
          wait=[("sdve", T("sdve"))], inc="sdve")
        E("dve", lambda v: v.tensor_mul(smf[:, 4:5], smf[:, 2:3], smf[:, 2:3]),
          wait=[("sdve", T("sdve"))], inc="sdve")
        E("dve", (lambda v: v.scalar_tensor_tensor(smf[:, 4:5], smf[:, 3:4], 1.0 / D, smf[:, 4:5],
                                                   ALU.mult, ALU.subtract)),
          wait=[("sact", t_sq), ("sdve", T("sdve"))], inc="sdve")
        E("dve", lambda v: v.tensor_scalar(smf[:, 4:5], smf[:, 4:5], EPS, None, ALU.add),
          wait=[("sdve", T("sdve"))], inc="sdve")
        t_var = T("sdve")
        E("act", (lambda s2: s2.activation(smf[:, 5:6], smf[:, 4:5], AF.Sqrt)),
          wait=[("sdve", t_var)], inc="sact")
        t_sqr = T("sact")
        E("dve", (lambda v: v.reciprocal(smf[:, 5:6], smf[:, 5:6])), wait=[("sact", t_sqr)],
          inc="sdve")
        t_ln = T("sdve")
        w = [("sdve", t_ln)] + list(extra_wait)
        E("dve", (lambda v, r_t=r_t, outbuf=outbuf:
                  v.tensor_scalar(outbuf, r_t, smf[:, 2:3], smf[:, 5:6], ALU.add, ALU.mult)),
          wait=w, inc="sdve")
        return T("sdve")

    for tb in range(NQB):
        bbb = tb % 2
        rsl = rr[tb % 2]

        w = [("qw", t_wo), ("spe", atl_used[bbb]), ("qad0", t_p2_ad0), ("qad1", t_p2_ad1),
             ("sdve", t_p2_dve)]
        E("sp", (lambda s, tb=tb, bbb=bbb: s.dma_start(out=atl[bbb], in_=adr[tb * P:(tb + 1) * P, :])),
          wait=w, inc="qsp", inc_by=16)
        t_al = T("qsp")
        w = [("sdve", xr_used[bbb])] if xr_used[bbb] else []
        E("sp", (lambda s, tb=tb, bbb=bbb:
                 s.dma_start(out=xrl[bbb], in_=xresq[tb * P:(tb + 1) * P, :])),
          wait=w, inc="qsp", inc_by=16)
        t_xr = T("qsp")

        afm = afm2[tb % 2]
        t_afm_parts = []
        for gr in range(4):
            w = [("qsp", t_al)]
            if pst_free[gr % 2]:
                w.append((pst_eng[gr % 2], pst_free[gr % 2]))
            if gr == 0 and afm_used[tb % 2]:
                w.append(("spe", afm_used[tb % 2]))
            tdst = ps_t if gr % 2 == 0 else alt_ps
            for icb in range(4):
                ic = gr * 4 + icb
                def tr(t, ic=ic, icb=icb, bbb=bbb, tdst=tdst):
                    return t.transpose(tdst[:, icb * P:(icb + 1) * P],
                                       atl[bbb][:, ic * P:(ic + 1) * P], idn[:])
                E("pe", tr, wait=(w if icb == 0 else ()), inc=("spe" if icb == 3 else None))
            t_tr = T("spe")
            if gr % 2 == 0:
                E("dve", (lambda v, gr=gr, afm=afm, tdst=tdst:
                          v.tensor_copy(afm[:, gr * 512:(gr + 1) * 512], tdst[:, 0:512])),
                  wait=[("spe", t_tr)], inc="sdve")
                pst_free[gr % 2] = T("sdve")
                pst_eng[gr % 2] = "sdve"
            else:
                E("act", (lambda s2, gr=gr, afm=afm, tdst=tdst:
                          s2.activation(afm[:, gr * 512:(gr + 1) * 512], tdst[:, 0:512], AF.Copy)),
                  wait=[("spe", t_tr)], inc="sact")
                pst_free[gr % 2] = T("sact")
                pst_eng[gr % 2] = "sact"
            t_afm_parts.append((pst_eng[gr % 2], pst_free[gr % 2]))
        atl_used[bbb] = T("spe")

        rw = [("qsp", r_free[tb % 2])] if r_free[tb % 2] else []
        for half in range(2):
            w = [(e_, v_) for (e_, v_) in t_afm_parts]
            if wo_ps_free[half]:
                w.append(("sdve", wo_ps_free[half]))
            for ic in range(NIC):
                for q2 in range(2):
                    def mm(t, ic=ic, q2=q2, half=half, afm=afm):
                        c0 = ic * D + half * 1024 + q2 * 512
                        return t.matmul(
                            ps_s[:, half * 1024 + q2 * 512:half * 1024 + (q2 + 1) * 512],
                            afm[:, ic * P:(ic + 1) * P], wa[:, c0:c0 + 512],
                            start=(ic == 0), stop=(ic == NIC - 1))
                    E("pe", mm, wait=(w if ic == 0 and q2 == 0 else ()),
                      inc=("spe" if ic == NIC - 1 and q2 == 1 else None))
            t_wo_mm = T("spe")
            if half == 1:
                afm_used[tb % 2] = t_wo_mm

            def res(v, half=half, bbb=bbb, rsl=rsl):
                return v.scalar_tensor_tensor(rsl[:, half * 1024:(half + 1) * 1024],
                                              ps_s[:, half * 1024:(half + 1) * 1024], 1.0,
                                              xrl[bbb][:, half * 1024:(half + 1) * 1024],
                                              ALU.mult, ALU.add,
                                              accum_out=smf[:, half:half + 1])
            E("dve", res, wait=[("spe", t_wo_mm), ("qsp", t_xr)] + rw, inc="sdve")
            wo_ps_free[half] = T("sdve")
        xr_used[bbb] = T("sdve")
        t_r1 = T("sdve")

        t_y1 = emit_ln(rsl, rsl, t_r1, [], "ln1")
        E("sp", (lambda s, tb=tb, rsl=rsl: s.dma_start(out=y1dr[tb * P:(tb + 1) * P, :], in_=rsl)),
          wait=[("sdve", t_y1)], inc="qsp", inc_by=16)
        r_free[tb % 2] = T("qsp")

    t_p3_pe = T("spe")

    t_wt = load_weight(wtdr, False, wait_extra=[("spe", t_p3_pe), ("qwt", T("qwt"))])

    y1l = [pf[:, 0:2048], pf[:, 2048:4096]]
    y1_used = list(xr_used)
    y1fm_used = [None, None]
    y1bf_used = None
    ffn_ps_free = None
    out_st = list(r_free)
    alt_ps4 = ps_s[:, 0:512].bitcast(BF16)

    for tb in range(NQB):
        bbb = tb % 2
        rsl = rr[tb % 2]

        w = [("qw", t_wt)]
        if y1_used[bbb]:
            w.append(("sdve", y1_used[bbb]))
        E("sp", (lambda s, tb=tb, bbb=bbb: s.dma_start(out=y1l[bbb], in_=y1dr[tb * P:(tb + 1) * P, :])),
          wait=w, inc="qsp", inc_by=16)
        t_yl = T("qsp")
        w = [("qsp", t_yl), ("sdve", T("sdve"))]
        if tb == 0:
            w.append(("spe", t_p3_pe))
        if y1bf_used:
            w.append(("spe", y1bf_used))
        E("dve", (lambda v, bbb=bbb: v.tensor_copy(y1bf[:], y1l[bbb])), wait=w, inc="sdve")
        t_y1b = T("sdve")

        y1fm = y1fm2[tb % 2]
        t_y1fm_parts = []
        for gr in range(4):
            w = [("sdve", t_y1b)]
            if pst_free[gr % 2]:
                w.append((pst_eng[gr % 2], pst_free[gr % 2]))
            if gr == 0 and y1fm_used[tb % 2]:
                w.append(("spe", y1fm_used[tb % 2]))
            tdst = ps_t if gr % 2 == 0 else alt_ps4
            for icb in range(4):
                ic = gr * 4 + icb
                def tr(t, ic=ic, icb=icb, tdst=tdst):
                    return t.transpose(tdst[:, icb * P:(icb + 1) * P],
                                       y1bf[:, ic * P:(ic + 1) * P], idn[:])
                E("pe", tr, wait=(w if icb == 0 else ()), inc=("spe" if icb == 3 else None))
            t_tr = T("spe")
            if gr % 2 == 0:
                E("dve", (lambda v, gr=gr, y1fm=y1fm, tdst=tdst:
                          v.tensor_copy(y1fm[:, gr * 512:(gr + 1) * 512], tdst[:, 0:512])),
                  wait=[("spe", t_tr)], inc="sdve")
                pst_free[gr % 2] = T("sdve")
                pst_eng[gr % 2] = "sdve"
            else:
                E("act", (lambda s2, gr=gr, y1fm=y1fm, tdst=tdst:
                          s2.activation(y1fm[:, gr * 512:(gr + 1) * 512], tdst[:, 0:512], AF.Copy)),
                  wait=[("spe", t_tr)], inc="sact")
                pst_free[gr % 2] = T("sact")
                pst_eng[gr % 2] = "sact"
            t_y1fm_parts.append((pst_eng[gr % 2], pst_free[gr % 2]))

        rw = [("qsp", out_st[tb % 2])] if out_st[tb % 2] else []
        for half in range(2):
            w = [(e_, v_) for (e_, v_) in t_y1fm_parts] + [("qsp", t_brow), ("qw", t_wt)]
            if ffn_ps_free:
                w.append(("sdve", ffn_ps_free))
            for ic in range(NIC):
                for q2 in range(2):
                    def mm(t, ic=ic, q2=q2, half=half, y1fm=y1fm):
                        c0 = ic * D + half * 1024 + q2 * 512
                        return t.matmul(ps_a[:, q2 * 512:(q2 + 1) * 512],
                                        y1fm[:, ic * P:(ic + 1) * P], wa[:, c0:c0 + 512],
                                        start=(ic == 0), stop=False)
                    E("pe", mm, wait=(w if ic == 0 and q2 == 0 else ()), inc=None)
            for q2 in range(2):
                def bm(t, q2=q2, half=half):
                    c0 = half * 1024 + q2 * 512
                    return t.matmul(ps_a[:, q2 * 512:(q2 + 1) * 512], ones_row_bf[:, :],
                                    brow[:, c0:c0 + 512], start=False, stop=True)
                E("pe", bm, inc=("spe" if q2 == 1 else None))
            t_ffn = T("spe")
            if half == 1:
                y1fm_used[tb % 2] = t_ffn
                y1bf_used = t_ffn

            def res2(v, half=half, bbb=bbb, rsl=rsl):
                return v.scalar_tensor_tensor(rsl[:, half * 1024:(half + 1) * 1024],
                                              ps_a[:], 1.0,
                                              y1l[bbb][:, half * 1024:(half + 1) * 1024],
                                              ALU.mult, ALU.add,
                                              accum_out=smf[:, half:half + 1])
            E("dve", res2, wait=[("spe", t_ffn)] + rw, inc="sdve")
            ffn_ps_free = T("sdve")
        y1_used[bbb] = T("sdve")
        t_r2 = T("sdve")

        t_out = emit_ln(rsl, rsl, t_r2, [], "ln2")
        E("sp", (lambda s, tb=tb, rsl=rsl: s.dma_start(out=y[tb * P:(tb + 1) * P, :], in_=rsl)),
          wait=[("sdve", t_out)], inc="qsp", inc_by=16)
        out_st[tb % 2] = T("qsp")

    E("sp", lambda s: s.nop(), wait=[("qsp", T("qsp"))])

    from contextlib import ExitStack
    sem_names = ["qsp", "spe", "sact", "sdve", "spool", "qc0", "qw",
                 "qx0", "qx1", "qa0", "qa1", "qh0", "qh1", "qad0", "qad1", "qwt"]
    with ExitStack() as stack:
        sems = {s: stack.enter_context(nc.semaphore(s)) for s in sem_names}
        block = stack.enter_context(nc.Block())

        def replay(name, eng):
            waited = {}
            for fn, waits, inc, n in sch.streams[name]:
                for (sm, val) in waits:
                    if val > waited.get(sm, 0):
                        eng.wait_ge(sems[sm], val)
                        waited[sm] = val
                inst = fn(eng)
                if inc is not None:
                    inst.then_inc(sems[inc], n)

        @block.tensor
        def _(t):
            replay("pe", t)

        @block.scalar
        def _(s):
            replay("act", s)

        @block.vector
        def _(v):
            replay("dve", v)

        @block.gpsimd
        def _(g):
            replay("pool", g)

        @block.sync
        def _(s):
            replay("sp", s)

    return nc


def _host_inputs(inputs, core):
    b = core // 2
    parity = core % 2
    stripe = stripe_list(parity)
    x = np.asarray(inputs["L0_attn_x"])
    xb = x[b]
    rows = np.concatenate([np.arange(g * P, (g + 1) * P) for g in stripe])
    dm = np.zeros((NQB, 2, P, P), dtype=ml_dtypes.bfloat16)
    kk = np.arange(P)[:, None]
    qq = np.arange(P)[None, :]
    for j in range(NQB):
        g = stripe[j]
        for r in range(2):
            kb = 2 * j + r
            dm[j, r] = ((kb * P + kk) <= (g * P + qq)).astype(ml_dtypes.bfloat16)
    vec = np.zeros((1, 16), dtype=np.float32)
    vec[0, 0:4] = np.asarray(inputs["L0_ffn_exps"])
    vec[0, 4:8] = np.asarray(inputs["L0_ffn_bexps"])
    vec[0, 8:12] = np.asarray(inputs["L0_ffn_mask_weight"])
    vec[0, 12] = np.asarray(inputs["L0_ffn_scale"]).reshape(-1)[0]
    vec[0, 13] = np.asarray(inputs["L0_ffn_biasscale"]).reshape(-1)[0]
    im = {
        "xTkv": np.ascontiguousarray(xb.T),
        "xqT": np.ascontiguousarray(xb[rows].T),
        "xresq": np.ascontiguousarray(xb[rows]),
        "wqT": np.ascontiguousarray(np.asarray(inputs["L0_attn_Wq"]).T),
        "wkT": np.ascontiguousarray(np.asarray(inputs["L0_attn_Wk"]).T),
        "wvT": np.ascontiguousarray(np.asarray(inputs["L0_attn_Wv"]).T),
        "woT": np.ascontiguousarray(np.asarray(inputs["L0_attn_Wo"]).T),
        "pw": np.ascontiguousarray(
            np.asarray(inputs["L0_ffn_pweight"]).transpose(0, 2, 1)).reshape(D, 4 * D),
        "nw": np.ascontiguousarray(
            np.asarray(inputs["L0_ffn_nweight"]).transpose(0, 2, 1)).reshape(D, 4 * D),
        "pbias": np.ascontiguousarray(np.asarray(inputs["L0_ffn_pbias"], dtype=np.float32)),
        "nbias": np.ascontiguousarray(np.asarray(inputs["L0_ffn_nbias"], dtype=np.float32)),
        "vecs": vec,
        "dmask": np.ascontiguousarray(dm.reshape(NQB * 2 * P, P)),
    }
    return im, b, rows


def kernel(**inputs):
    nc = build_program()
    in_maps = []
    meta = []
    for c in range(8):
        im, b, rows = _host_inputs(inputs, c)
        in_maps.append(im)
        meta.append((b, rows))
    res = run_bass_kernel_spmd(nc, in_maps, core_ids=list(range(8)))
    out = np.zeros((B, S, D), dtype=np.float32)
    for c in range(8):
        b, rows = meta[c]
        out[b, rows] = res.results[c]["y"]
    return out


# version 15
# speedup vs baseline: 1.3756x; 1.3756x over previous
import numpy as np
import ml_dtypes

import concourse.bass as bass
import concourse.mybir as mybir
from concourse.bass_utils import run_bass_kernel_spmd

F32 = mybir.dt.float32
BF16 = mybir.dt.bfloat16
I32 = mybir.dt.int32
AF = mybir.ActivationFunctionType
ALU = mybir.AluOpType

P = 128
D = 2048
S = 4096
B = 4
H = 16
TOK = 2048
NQB = 16
NIC = D // P
INV_SQRT_DK = float(1.0 / np.sqrt(128.0))
EPS = 1e-5

ENGINES = ("pe", "act", "dve", "pool", "sp")


def stripe_list(parity):
    if parity == 0:
        return [g for g in range(2 * NQB) if g % 4 in (0, 3)]
    return [g for g in range(2 * NQB) if g % 4 in (1, 2)]


class Sched:

    def __init__(self):
        self.streams = {e: [] for e in ENGINES}
        self.cnt = {}

    def emit(self, eng, fn, wait=(), inc=None, inc_by=1):
        tok = None
        if inc is not None:
            self.cnt[inc] = self.cnt.get(inc, 0) + inc_by
            tok = self.cnt[inc]
        w = tuple((s, v) for (s, v) in wait if v)
        self.streams[eng].append((fn, w, inc, inc_by))
        return tok

    def tok(self, sem):
        return self.cnt.get(sem, 0)


def build_program(debug_taps=False):
    nc = bass.Bass("TRN2", target_bir_lowering=False)

    xTkv = nc.dram_tensor("xTkv", [D, S], F32, kind="ExternalInput").ap()
    xqT = nc.dram_tensor("xqT", [D, TOK], F32, kind="ExternalInput").ap()
    xresq = nc.dram_tensor("xresq", [TOK, D], F32, kind="ExternalInput").ap()
    wqT = nc.dram_tensor("wqT", [D, D], F32, kind="ExternalInput").ap()
    wkT = nc.dram_tensor("wkT", [D, D], F32, kind="ExternalInput").ap()
    wvT = nc.dram_tensor("wvT", [D, D], F32, kind="ExternalInput").ap()
    woT = nc.dram_tensor("woT", [D, D], F32, kind="ExternalInput").ap()
    pw = nc.dram_tensor("pw", [D, 4 * D], F32, kind="ExternalInput").ap()
    nw = nc.dram_tensor("nw", [D, 4 * D], F32, kind="ExternalInput").ap()
    pbias = nc.dram_tensor("pbias", [D, 4], F32, kind="ExternalInput").ap()
    nbias = nc.dram_tensor("nbias", [D, 4], F32, kind="ExternalInput").ap()
    vecs = nc.dram_tensor("vecs", [1, 16], F32, kind="ExternalInput").ap()
    dmask = nc.dram_tensor("dmask", [NQB * 2 * P, P], BF16, kind="ExternalInput").ap()

    y = nc.dram_tensor("y", [TOK, D], F32, kind="ExternalOutput").ap()

    KIND = "ExternalOutput" if debug_taps else "Internal"
    kdr = nc.dram_tensor("kdr", [D, S], BF16, kind=KIND).ap()
    vdr = nc.dram_tensor("vdr", [P, H * S // P * P], BF16, kind=KIND).ap()
    qdr = nc.dram_tensor("qdr", [D, TOK], BF16, kind=KIND).ap()
    adr = nc.dram_tensor("adr", [D, TOK], BF16, kind=KIND).ap()
    wtdr = nc.dram_tensor("wtdr", [D, D], BF16, kind=KIND).ap()
    y1dr = nc.dram_tensor("y1dr", [TOK, D], F32, kind=KIND).ap()
    bdr = nc.dram_tensor("bdr", [1, D], BF16, kind=KIND).ap()

    ident_c = nc.inline_tensor(np.eye(P, dtype=ml_dtypes.bfloat16), name="identc").ap()
    onesc_c = nc.inline_tensor(np.ones((P, 1), dtype=ml_dtypes.bfloat16), name="onescc").ap()
    onesr_c = nc.inline_tensor(np.ones((1, P), dtype=np.float32), name="onesrc").ap()
    onesrb_c = nc.inline_tensor(np.ones((1, P), dtype=ml_dtypes.bfloat16), name="onesrbc").ap()

    wa = nc.alloc_sbuf_tensor("wa", [P, NIC * D], BF16)
    pbf = nc.alloc_sbuf_tensor("pbf", [P, 16384], BF16)
    pf = nc.alloc_sbuf_tensor("pf", [P, 10240], F32)
    wstg = nc.alloc_sbuf_tensor("wstg", [P, 4096], BF16)
    sh2k = nc.alloc_sbuf_tensor("sh2k", [P, 2048], F32)
    stg = nc.alloc_sbuf_tensor("stg", [P, 4096], BF16)
    msk = nc.alloc_sbuf_tensor("msk", [P, NQB * 2 * P], BF16)
    idn = nc.alloc_sbuf_tensor("idn", [P, P], BF16)
    ones_col = nc.alloc_sbuf_tensor("ones_col", [P, 1], BF16)
    ones_row = nc.alloc_sbuf_tensor("ones_row", [1, P], F32)
    ones_row_bf = nc.alloc_sbuf_tensor("ones_row_bf", [1, P], BF16)
    vrow = nc.alloc_sbuf_tensor("vrow", [1, 32], F32)
    brow = nc.alloc_sbuf_tensor("brow", [1, D], BF16)
    lrow = nc.alloc_sbuf_tensor("lrow", [1, 1024], F32)
    bcsb = nc.alloc_sbuf_tensor("bcsb", [P, 1024], F32)
    astg = nc.alloc_sbuf_tensor("astg", [P, 1024], BF16)
    zb = nc.alloc_sbuf_tensor("zb", [P, 1], F32)
    pbv = nc.alloc_sbuf_tensor("pbv", [P, 128], F32)
    bcol = nc.alloc_sbuf_tensor("bcol", [P, 16], F32)
    bcol_b = nc.alloc_sbuf_tensor("bcol_b", [P, 16], BF16)
    cbc = nc.alloc_sbuf_tensor("cbc", [P, 16], F32)
    smf = nc.alloc_sbuf_tensor("smf", [P, 16], F32)
    woq0 = nc.alloc_sbuf_tensor("woq0", [P, 4096], BF16)

    ps_s = nc.alloc_psum_tensor("ps_s", [P, 2048], F32)
    ps_a = nc.alloc_psum_tensor("ps_a", [P, 1024], F32)
    ps_t = nc.alloc_psum_tensor("ps_t", [P, 1024], BF16)
    ps_l = nc.alloc_psum_tensor("ps_l", [P, 512], F32)

    sch = Sched()
    E = sch.emit
    T = sch.tok

    E("pool", lambda g: g.dma_start(out=idn[:], in_=ident_c[:, :]), inc="qc0", inc_by=16)
    E("pool", lambda g: g.dma_start(out=ones_col[:], in_=onesc_c[:, :]), inc="qc0", inc_by=16)
    E("pool", lambda g: g.dma_start(out=ones_row[:], in_=onesr_c[:, :]), inc="qc0", inc_by=16)
    E("pool", lambda g: g.dma_start(out=ones_row_bf[:], in_=onesrb_c[:, :]), inc="qc0", inc_by=16)
    E("pool", lambda g: g.dma_start(out=vrow[:, 0:16], in_=vecs[:, :]), inc="qc0", inc_by=16)
    E("pool", lambda g: g.dma_start(
        out=msk[:].rearrange("p (j r q) -> p j r q", j=NQB, r=2),
        in_=dmask.rearrange("(j r p) q -> p j r q", j=NQB, r=2)), inc="qc0", inc_by=16)
    E("pool", lambda g: g.dma_start(
        out=pbv[:, 0:64].rearrange("p (t n) -> p t n", t=NIC),
        in_=pbias.rearrange("(t p) n -> p t n", p=P),
        allow_slow_non_contiguous=True), inc="qc0", inc_by=16)
    E("pool", lambda g: g.dma_start(
        out=pbv[:, 64:128].rearrange("p (t n) -> p t n", t=NIC),
        in_=nbias.rearrange("(t p) n -> p t n", p=P),
        allow_slow_non_contiguous=True), inc="qc0", inc_by=16)
    t_p0 = T("qc0")
    E("dve", lambda v: v.memset(zb[:], 0.0), wait=[("qc0", t_p0)], inc="sdve")

    E("act", lambda s: s.activation(vrow[:1, 16:20], vrow[:1, 8:12], AF.Sigmoid),
      wait=[("qc0", t_p0)], inc="sact")
    t_sg = T("sact")
    E("dve", lambda v: v.tensor_mul(vrow[:1, 16:20], vrow[:1, 16:20], vrow[:1, 0:4]),
      wait=[("sact", t_sg)], inc="sdve")
    E("dve", lambda v: v.tensor_scalar(vrow[:1, 16:20], vrow[:1, 16:20], vrow[:1, 12:13], None,
                                       ALU.mult), wait=[("sdve", T("sdve"))], inc="sdve")
    E("dve", lambda v: v.tensor_scalar(vrow[:1, 20:24], vrow[:1, 16:20], -1.0, None, ALU.mult),
      wait=[("sdve", T("sdve"))], inc="sdve")
    E("dve", lambda v: v.tensor_copy(vrow[:1, 24:28], vrow[:1, 4:8]), inc="sdve")
    E("dve", lambda v: v.tensor_copy(vrow[:1, 28:29], vrow[:1, 13:14]), inc="sdve")
    t_vr = T("sdve")
    t_bco = E("pe", lambda t: t.matmul(ps_l[:, 16:32], ones_row[:, :], vrow[:1, 16:32],
                                       start=True, stop=True),
              wait=[("sdve", t_vr), ("qc0", t_p0)], inc="spe")
    E("dve", lambda v: v.tensor_copy(cbc[:], ps_l[:, 16:32]), wait=[("spe", t_bco)], inc="sdve")
    t_cbc = T("sdve")

    E("dve", lambda v: v.tensor_sub(pbv[:, 0:64], pbv[:, 0:64], pbv[:, 64:128]),
      wait=[("sdve", t_cbc)], inc="sdve")

    def _braw(n):
        def f(v, n=n):
            d3 = pbv[:, 0:64].rearrange("p (t n) -> p t n", t=NIC)[:, :, n]
            if n == 0:
                return v.tensor_scalar(bcol[:], d3, cbc[:, 8 + n:9 + n], None, ALU.mult)
            return v.scalar_tensor_tensor(bcol[:], d3, cbc[:, 8 + n:9 + n], bcol[:],
                                          ALU.mult, ALU.add)
        return f
    for n in range(4):
        E("dve", _braw(n), wait=[("sdve", T("sdve"))], inc="sdve")
    E("dve", lambda v: v.tensor_scalar(bcol[:], bcol[:], 1.0, -1.0, ALU.min, ALU.max),
      wait=[("sdve", T("sdve"))], inc="sdve")
    E("dve", lambda v: v.tensor_scalar(bcol[:], bcol[:], 15.0, None, ALU.mult),
      wait=[("sdve", T("sdve"))], inc="sdve")
    E("dve", lambda v: v.tensor_scalar(bcol[:], bcol[:], 12582912.0, -12582912.0,
                                       ALU.add, ALU.add), wait=[("sdve", T("sdve"))], inc="sdve")
    E("dve", lambda v: v.tensor_scalar(bcol[:], bcol[:], 1.0 / 15.0, None, ALU.mult),
      wait=[("sdve", T("sdve"))], inc="sdve")
    E("dve", lambda v: v.tensor_scalar(bcol_b[:], bcol[:], cbc[:, 12:13], None, ALU.mult),
      wait=[("sdve", T("sdve"))], inc="sdve")
    t_bc = T("sdve")
    t_btr = E("pe", lambda t: t.transpose(ps_t[:16, 0:P], bcol_b[:], idn[:]),
              wait=[("sdve", t_bc), ("qc0", t_p0)], inc="spe")
    E("dve", lambda v: v.tensor_copy(stg[:16, 0:P], ps_t[:16, 0:P]), wait=[("spe", t_btr)],
      inc="sdve")
    E("sp", lambda s: s.dma_start(out=bdr[:, :].rearrange("one (t p) -> (one t) p", t=16),
                                  in_=stg[:16, 0:P]),
      wait=[("sdve", T("sdve"))], inc="qsp", inc_by=16)
    E("sp", lambda s: s.dma_start(out=brow[:, :], in_=bdr[:, :]),
      wait=[("qsp", T("qsp"))], inc="qsp", inc_by=16)
    t_brow = T("qsp")

    xck = [pbf[:, 0:8192], pbf[:, 8192:16384]]

    def xc3(buf):
        return buf.rearrange("p (c t) -> p c t", c=NIC)

    QSEM = {"pool": "qw", "sp": "qsp", "act": "qa0"}

    def _wq_quarter(g, quarter, dram_w):
        return g.dma_start(
            out=wa[:, quarter * 4 * D:(quarter + 1) * 4 * D]
            .rearrange("p (c o) -> p c o", c=4),
            in_=dram_w[quarter * 512:(quarter + 1) * 512, :]
            .rearrange("(c p) o -> p c o", p=P))

    def load_weight(dram_w, wait_extra=()):
        for quarter in range(4):
            E("pool", (lambda g, quarter=quarter, dram_w=dram_w:
                       _wq_quarter(g, quarter, dram_w)),
              wait=(list(wait_extra) if quarter == 0 else []), inc="qw", inc_by=16)
        return [("qw", T("qw"))]

    def load_weight_bf(dram_w, queues, wait_extra=(), quarters=(0, 1, 2, 3)):
        first_on_queue = set()
        toks = {}
        for qi, quarter in enumerate(quarters):
            qq = queues[qi % len(queues)]

            def f(g, quarter=quarter, dram_w=dram_w):
                return g.dma_start(
                    out=wa[:, quarter * 4 * D:(quarter + 1) * 4 * D]
                    .rearrange("p (c o) -> p c o", c=4),
                    in_=dram_w[quarter * 512:(quarter + 1) * 512, :]
                    .rearrange("(c p) o -> p c o", p=P))
            w = list(wait_extra) if qq not in first_on_queue else []
            first_on_queue.add(qq)
            E(qq, f, wait=w, inc=QSEM[qq], inc_by=16)
            toks[QSEM[qq]] = T(QSEM[qq])
        return list(toks.items())

    def load_weight_cols(dram_w, half, wait_tok):
        o0 = half * 1024

        def f(g, o0=o0, dram_w=dram_w):
            return g.dma_start(
                out=wa[:, :].rearrange("p (c o) -> p c o", c=NIC)[:, :, o0:o0 + 1024],
                in_=dram_w[:, o0:o0 + 1024].rearrange("(c p) o -> p c o", p=P))
        E("pool", f, wait=[("spe", wait_tok)], inc="qw", inc_by=16)
        return [("qw", T("qw"))]

    pslot = [ps_s[:, i * 512:(i + 1) * 512] for i in range(4)]
    slot_free = [None, None, None, None]
    xbuf_free = [None, None]

    pwb = [pf[:, i * 1024:(i + 1) * 1024] for i in range(4)]
    nwb = [pf[:, 4096 + i * 1024:4096 + (i + 1) * 1024] for i in range(4)]
    acc = pf[:, 8192:10240]
    sgsh = [sh2k[:, 0:1024], sh2k[:, 1024:2048]]

    wb_sched = [("load", 0)]
    for q in range(128):
        if q + 1 < 128:
            wb_sched.append(("load", q + 1))
        wb_sched.append(("comp", q))
        if q % 8 == 7:
            wb_sched.append(("fin", q // 8))
    wb_state = {
        "pw_free": [None] * 4, "nw_free": [None] * 4,
        "ld_p": [None] * 128, "ld_n": [None] * 128,
        "acc_tok": [None, None], "sgs_tok": [None, None],
        "wt_dma": None, "pst_tok": None, "idx": 0,
    }

    def wstep():
        st = wb_state
        if st["idx"] >= len(wb_sched):
            return
        kind, q = wb_sched[st["idx"]]
        st["idx"] += 1
        if kind == "load":
            sl, nh = q // 8, q % 8
            n, ih = nh // 2, nh % 2
            bb = q % 4
            col0 = n * D + ih * 1024
            qs = "qa%d" % (q % 2)
            w = [("sact", st["pw_free"][bb])] if st["pw_free"][bb] else []
            E("act", (lambda s, sl=sl, col0=col0, bb=bb:
                      s.dma_start(out=pwb[bb], in_=pw[sl * P:(sl + 1) * P, col0:col0 + 1024])),
              wait=w, inc=qs, inc_by=16)
            w = [("sact", st["nw_free"][bb])] if st["nw_free"][bb] else []
            E("act", (lambda s, sl=sl, col0=col0, bb=bb:
                      s.dma_start(out=nwb[bb], in_=nw[sl * P:(sl + 1) * P, col0:col0 + 1024])),
              wait=w, inc=qs, inc_by=16)
            st["ld_p"][q] = T(qs)
            st["ld_n"][q] = T(qs)
        elif kind == "comp":
            sl, nh = q // 8, q % 8
            n, ih = nh // 2, nh % 2
            bb = q % 4
            sgp = q % 2
            sgs = sgsh[sgp]
            accsl = acc[:, ih * 1024:(ih + 1) * 1024]
            w = [("qa%d" % (q % 2), st["ld_n"][q]), ("sdve", t_cbc)]
            if st["sgs_tok"][sgp]:
                w.append(("sdve", st["sgs_tok"][sgp]))
            E("act", (lambda s, bb=bb, sgs=sgs: s.activation(sgs, pwb[bb], AF.Sigmoid)),
              wait=w, inc="sact")
            st["pw_free"][bb] = T("sact")

            def stt_p(v, n=n, accsl=accsl, sgs=sgs):
                if n == 0:
                    return v.tensor_scalar(accsl, sgs, cbc[:, 0:1], None, ALU.mult)
                return v.scalar_tensor_tensor(accsl, sgs, cbc[:, n:n + 1], accsl, ALU.mult, ALU.add)
            w = [("sact", st["pw_free"][bb])]
            if st["acc_tok"][ih]:
                w.append(("sdve", st["acc_tok"][ih]))
            E("dve", stt_p, wait=w, inc="sdve")
            t_sttp = T("sdve")
            E("act", (lambda s, bb=bb, sgs=sgs: s.activation(sgs, nwb[bb], AF.Sigmoid)),
              wait=[("qa%d" % (q % 2), st["ld_n"][q]), ("sdve", t_sttp)], inc="sact")
            st["nw_free"][bb] = T("sact")

            def stt_n(v, n=n, accsl=accsl, sgs=sgs):
                return v.scalar_tensor_tensor(accsl, sgs, cbc[:, 4 + n:5 + n], accsl,
                                              ALU.mult, ALU.add)
            E("dve", stt_n, wait=[("sact", st["nw_free"][bb]), ("sdve", t_sttp)], inc="sdve")
            st["acc_tok"][ih] = T("sdve")
            st["sgs_tok"][sgp] = T("sdve")
        else:
            sl = q
            w = [("sdve", st["acc_tok"][0] or 0), ("sdve", st["acc_tok"][1] or 0)]
            if st["wt_dma"]:
                w.append(("qwt", st["wt_dma"]))
            E("dve", lambda v: v.tensor_copy(wstg[:, 0:2048], acc),
              wait=w + [("sdve", T("sdve"))], inc="sdve")
            t_wbf = T("sdve")
            for gr in range(2):
                w = [("sdve", t_wbf)]
                if st["pst_tok"]:
                    w.append(("sdve", st["pst_tok"]))
                for icb in range(8):
                    ic = gr * 8 + icb
                    def tr(t, ic=ic, icb=icb):
                        return t.transpose(ps_t[:, icb * P:(icb + 1) * P],
                                           wstg[:, ic * P:(ic + 1) * P], idn[:])
                    E("pe", tr, wait=(w if icb == 0 else ()), inc=("spe" if icb == 7 else None))
                t_tr = T("spe")
                E("dve", (lambda v, gr=gr:
                          v.tensor_copy(wstg[:, 2048 + gr * 1024:2048 + (gr + 1) * 1024], ps_t[:])),
                  wait=[("spe", t_tr)], inc="sdve")
                st["pst_tok"] = T("sdve")

            def wt_out(s, sl=sl):
                return s.dma_start(
                    out=wtdr.rearrange("(ic p) o -> p ic o", p=P)[:, :, sl * P:(sl + 1) * P],
                    in_=wstg[:, 2048:4096].rearrange("p (ic o) -> p ic o", ic=NIC))
            E("sp", wt_out, wait=[("sdve", st["pst_tok"])], inc="qwt", inc_by=16)
            st["wt_dma"] = T("qwt")

    def wdrain():
        while wb_state["idx"] < len(wb_sched):
            wstep()

    t_wk_h = [load_weight_cols(wkT, 0, 0), None]

    def lx0(g):
        return g.dma_start(out=xck[0].rearrange("p (c t) -> p c t", c=NIC),
                           in_=xTkv[:, 0:512].rearrange("(c p) t -> p c t", p=P))
    E("pool", lx0, inc="qx0", inc_by=16)
    klx0 = T("qx0")
    t_wk_h[1] = load_weight_cols(wkT, 1, 0)
    t_k_half0 = None
    for tc in range(8):
        bb = tc % 2
        if tc == 0:
            t_x = klx0
        else:
            w = []
            if xbuf_free[bb] is not None:
                w.append(("spe", xbuf_free[bb]))

            def lx(g, tc=tc, bb=bb):
                return g.dma_start(out=xck[bb].rearrange("p (c t) -> p c t", c=NIC),
                                   in_=xTkv[:, tc * 512:(tc + 1) * 512]
                                   .rearrange("(c p) t -> p c t", p=P))
            E("pool", lx, wait=w, inc="qx%d" % bb, inc_by=16)
            t_x = T("qx%d" % bb)
        for oc in range(NIC):
            sl = (tc * NIC + oc) % 4
            stgsl = stg[:, sl * 512:(sl + 1) * 512]
            w = [("qx%d" % bb, t_x)] + list(t_wk_h[oc // 8])
            if slot_free[sl] is not None:
                w.append(("sdve", slot_free[sl]))
            for ic in range(NIC):
                def mm(t, ic=ic, oc=oc, bb=bb, sl=sl):
                    src = xck[bb].rearrange("p (c t) -> p c t", c=NIC)[:, ic, :]
                    return t.matmul(pslot[sl],
                                    wa[:, ic * D + oc * P:ic * D + (oc + 1) * P],
                                    src, start=(ic == 0), stop=(ic == NIC - 1))
                E("pe", mm, wait=(w if ic == 0 else ()), inc=("spe" if ic == NIC - 1 else None))
            t_mm = T("spe")
            if tc == 7 and oc == 7:
                t_k_half0 = t_mm
            if oc == NIC - 1:
                xbuf_free[bb] = t_mm
            E("dve", (lambda v, sl=sl, stgsl=stgsl: v.tensor_copy(stgsl, pslot[sl])),
              wait=[("spe", t_mm)], inc="sdve")
            slot_free[sl] = T("sdve")

            def kout(s, oc=oc, tc=tc, stgsl=stgsl):
                return s.dma_start(out=kdr[oc * P:(oc + 1) * P, tc * 512:(tc + 1) * 512], in_=stgsl)
            E("sp", kout, wait=[("sdve", slot_free[sl])], inc="qsp", inc_by=16)
            wstep()

    def lx_kv(tc, bb):
        w = []
        if xbuf_free[bb] is not None:
            w.append(("spe", xbuf_free[bb]))

        def lx(g, tc=tc, bb=bb):
            return g.dma_start(out=xck[bb].rearrange("p (c t) -> p c t", c=NIC),
                               in_=xTkv[:, tc * 512:(tc + 1) * 512]
                               .rearrange("(c p) t -> p c t", p=P))
        E("pool", lx, wait=w, inc="qx%d" % bb, inc_by=16)
        return T("qx%d" % bb)

    t_wv_h = [load_weight_cols(wvT, 0, t_k_half0), None]
    vlx0 = lx_kv(0, 0)
    t_wv_h[1] = load_weight_cols(wvT, 1, T("spe"))
    t_v_half0 = None
    for tc in range(8):
        bb = tc % 2
        t_x = vlx0 if tc == 0 else lx_kv(tc, bb)
        for tb in range(4):
            for op in range(4):
                sl = (tb * 4 + op) % 4
                stgsl = stg[:, sl * 512:(sl + 1) * 512]
                w = [("qx%d" % bb, t_x)] + list(t_wv_h[op // 2])
                if slot_free[sl] is not None:
                    w.append(("sdve", slot_free[sl]))
                for ic in range(NIC):
                    def mm(t, ic=ic, tb=tb, op=op, bb=bb, sl=sl):
                        src = xck[bb].rearrange("p (c t) -> p c t", c=NIC)[:, ic, tb * P:(tb + 1) * P]
                        return t.matmul(pslot[sl], src,
                                        wa[:, ic * D + op * 512:ic * D + (op + 1) * 512],
                                        start=(ic == 0), stop=(ic == NIC - 1))
                    E("pe", mm, wait=(w if ic == 0 else ()), inc=("spe" if ic == NIC - 1 else None))
                t_mm = T("spe")
                if tc == 7 and tb == 3 and op == 1:
                    t_v_half0 = t_mm
                if tb == 3 and op == 3:
                    xbuf_free[bb] = t_mm
                E("dve", (lambda v, sl=sl, stgsl=stgsl: v.tensor_copy(stgsl, pslot[sl])),
                  wait=[("spe", t_mm)], inc="sdve")
                slot_free[sl] = T("sdve")

                def vout(s, tc=tc, tb=tb, op=op, stgsl=stgsl):
                    blk = tc * 4 + tb
                    dst = vdr.rearrange("p (h b v) -> p h b v", h=H, b=S // P)
                    return s.dma_start(
                        out=dst[:, op * 4:(op + 1) * 4, blk, :],
                        in_=stgsl.rearrange("p (h v) -> p h v", h=4))
                E("sp", vout, wait=[("sdve", slot_free[sl])], inc="qsp", inc_by=16)
                wstep()

    def lx_q(qc, bb):
        w = []
        if xbuf_free[bb] is not None:
            w.append(("spe", xbuf_free[bb]))

        def lx(g, qc=qc, bb=bb):
            return g.dma_start(out=xck[bb].rearrange("p (c t) -> p c t", c=NIC),
                               in_=xqT[:, qc * 512:(qc + 1) * 512]
                               .rearrange("(c p) t -> p c t", p=P))
        E("pool", lx, wait=w, inc="qx%d" % bb, inc_by=16)
        return T("qx%d" % bb)

    t_wq_h = [load_weight_cols(wqT, 0, t_v_half0), None]
    qlx0 = lx_q(0, 0)
    t_wq_h[1] = load_weight_cols(wqT, 1, T("spe"))
    for qc in range(4):
        bb = qc % 2
        t_x = qlx0 if qc == 0 else lx_q(qc, bb)
        for oc in range(NIC):
            sl = (qc * NIC + oc) % 4
            stgsl = stg[:, sl * 512:(sl + 1) * 512]
            w = [("qx%d" % bb, t_x)] + list(t_wq_h[oc // 8])
            if slot_free[sl] is not None:
                w.append(("sdve", slot_free[sl]))
            for ic in range(NIC):
                def mm(t, ic=ic, oc=oc, bb=bb, sl=sl):
                    src = xck[bb].rearrange("p (c t) -> p c t", c=NIC)[:, ic, :]
                    return t.matmul(pslot[sl],
                                    wa[:, ic * D + oc * P:ic * D + (oc + 1) * P],
                                    src, start=(ic == 0), stop=(ic == NIC - 1))
                E("pe", mm, wait=(w if ic == 0 else ()), inc=("spe" if ic == NIC - 1 else None))
            t_mm = T("spe")
            if oc == NIC - 1:
                xbuf_free[bb] = t_mm
            E("dve", (lambda v, sl=sl, stgsl=stgsl:
                      v.tensor_scalar(stgsl, pslot[sl], INV_SQRT_DK, None, ALU.mult)),
              wait=[("spe", t_mm)], inc="sdve")
            slot_free[sl] = T("sdve")

            def qout(s, oc=oc, qc=qc, stgsl=stgsl):
                return s.dma_start(out=qdr[oc * P:(oc + 1) * P, qc * 512:(qc + 1) * 512], in_=stgsl)
            E("sp", qout, wait=[("sdve", slot_free[sl])], inc="qsp", inc_by=16)
            if qc == 3 and oc == 1:
                t_q01_st = T("qsp")
            wstep()
            wstep()
    t_xck0_free = xbuf_free[0]
    wdrain()

    qb_ = [pbf[:, 0:2048], pbf[:, 2048:4096]]
    kb_ = [pbf[:, 4096:8192], pbf[:, 8192:12288]]
    vb_ = [wa[:, 0:4096], wa[:, 4096:8192]]
    ptb = [pbf[:, 12288 + i * 1024:12288 + (i + 1) * 1024] for i in range(4)]

    t_p1_pe = T("spe")
    t_p1_sp = T("qsp")
    t_p1_dve = T("sdve")

    wo01 = wstg
    wo23 = stg
    wt01 = pf[:, 8192:10240].bitcast(BF16)
    wt23 = woq0

    def _q0_load(g, dram_w, half, dstbuf):
        return g.dma_start(
            out=dstbuf[:, :].rearrange("p (c o) -> p c o", c=2),
            in_=dram_w[half * 256:(half + 1) * 256, :].rearrange("(c p) o -> p c o", p=P))
    E("pool", (lambda g: _q0_load(g, woT, 0, wo01)),
      wait=[("qwt", T("qwt")), ("qsp", t_p1_sp)], inc="qw", inc_by=16)
    E("pool", (lambda g: _q0_load(g, woT, 1, wo23)), inc="qw", inc_by=16)
    E("pool", (lambda g: g.dma_start(
        out=wt01[:, :].rearrange("p (c o) -> p c o", c=2),
        in_=wtdr[0:256, :].rearrange("(c p) o -> p c o", p=P))),
      wait=[("sdve", t_p1_dve)], inc="qwt", inc_by=16)
    E("pool", (lambda g: g.dma_start(
        out=wt23[:, :].rearrange("p (c o) -> p c o", c=2),
        in_=wtdr[256:512, :].rearrange("(c p) o -> p c o", p=P))),
      inc="qwt", inc_by=16)
    t_wt_q0 = T("qwt")
    for quarter in (1, 2, 3):
        E("pool", (lambda g, quarter=quarter: _wq_quarter(g, quarter, woT)),
          wait=([("spe", t_p1_pe)] if quarter == 1 else []), inc="qw", inc_by=16)
    t_wo_early = T("qw")

    q_used = [t_p1_pe, t_p1_pe]
    k_used = [t_p1_pe, t_p1_pe]
    v_used = [t_p1_pe, t_p1_pe]
    hl_qk = [None] * H
    hl_v = [None] * H

    def emit_head_loads(h, qk_wait=None, v_wait=None):
        bb = h % 2
        qs = "qh%d" % bb
        w = (qk_wait if qk_wait is not None else
             [("qsp", t_p1_sp), ("spe", max(q_used[bb], k_used[bb], v_used[bb]))])
        E("sp", (lambda s, h=h, bb=bb: s.dma_start(out=qb_[bb], in_=qdr[h * P:(h + 1) * P, :])),
          wait=w, inc=qs, inc_by=16)
        E("sp", (lambda s, h=h, bb=bb: s.dma_start(out=kb_[bb], in_=kdr[h * P:(h + 1) * P, :])),
          inc=qs, inc_by=16)
        hl_qk[h] = T(qs)
        E("sp", (lambda s, h=h, bb=bb:
                 s.dma_start(out=vb_[bb][:, :], in_=vdr[:, h * 4096:(h + 1) * 4096])),
          wait=(v_wait or []), inc=qs, inc_by=16)
        hl_v[h] = T(qs)

    emit_head_loads(0, qk_wait=[("qsp", t_q01_st), ("spe", t_xck0_free)],
                    v_wait=[("spe", t_p1_pe)])
    emit_head_loads(1, qk_wait=[("qsp", t_q01_st), ("spe", t_p1_pe)])

    sb_s = [ps_s[:, 0:1024], ps_s[:, 1024:2048]]
    s_free = [[None, None], [None, None]]
    pt_rd = [None, None, None, None]
    pt_rd_dve = [None, None, None, None]
    a_free = [None, None]
    astg_free = [None, None]

    ps6f = ps_t[:, 0:1024].bitcast(F32)
    den_row = [ps6f[0:1, 0:512], ps_l[0:1, 0:512]]
    bc_ps = [ps6f[:, 0:512], ps_l[:, 0:512]]

    hsb = 0
    pending_fin = [None]

    def emit_finalize(fx):
        ab = fx["ab"]
        astg_b = fx["astg_b"]; a_ps = fx["a_ps"]
        lr = lrow[:, ab * 512:(ab + 1) * 512]
        bsb = bcsb[:, ab * 512:(ab + 1) * 512]
        E("dve", (lambda v, lr=lr, ab=ab: v.reciprocal(lr, den_row[ab])),
          wait=[("spe", fx["t_pv_done"]), ("sdve", T("sdve"))], inc="sdve")
        t_rc = T("sdve")
        E("pe", (lambda t, lr=lr, ab=ab: t.matmul(bc_ps[ab], ones_row[:, :], lr,
                                                  start=True, stop=True)),
          wait=[("sdve", t_rc)], inc="spe")
        t_bc = T("spe")
        E("dve", (lambda v, bsb=bsb, ab=ab: v.tensor_copy(bsb, bc_ps[ab])),
          wait=[("spe", t_bc), ("sdve", T("sdve"))], inc="sdve")
        t_ev = T("sdve")
        w = [("sdve", t_ev)]
        if astg_free[ab] is not None:
            w.append(("qad%d" % ab, astg_free[ab]))
        E("dve", (lambda v, astg_b=astg_b, a_ps=a_ps, bsb=bsb:
                  v.tensor_tensor(astg_b, a_ps, bsb, ALU.mult)), wait=w, inc="sdve")
        a_free[ab] = T("sdve")

        def aout(g2, h=fx["h"], sb=fx["sb"], astg_b=astg_b):
            return g2.dma_start(out=adr[h * P:(h + 1) * P, sb * 512:(sb + 1) * 512],
                                in_=astg_b)
        E("sp", aout, wait=[("sdve", a_free[ab])], inc="qad%d" % ab, inc_by=16)
        astg_free[ab] = T("qad%d" % ab)

    for h in range(H):
        bb = h % 2
        for sb in range(4):
            ab = hsb % 2
            npairs = 4 * (sb + 1)
            astg_b = astg[:, ab * 512:(ab + 1) * 512]
            a_ps = ps_a[:, ab * 512:(ab + 1) * 512]

            pending_pv = None

            def flush_pv():
                nonlocal pending_pv
                if pending_pv is None:
                    return
                ops, ptv_, is_last = pending_pv
                nops = len(ops)
                for r, (fn, w_) in enumerate(ops):
                    E("pe", fn, wait=w_, inc=("spe" if r == nops - 1 else None))
                pt_rd[ptv_] = T("spe")
                if is_last:
                    v_used[bb] = max(v_used[bb], pt_rd[ptv_])
                pending_pv = None

            for g in range(npairs):
                sbuf = g % 2
                off = max(0, g - 4 * sb)
                boundary = 4 * sb <= g <= 4 * sb + 3

                t_qk = [None, None]
                for r in range(2):
                    kb = 2 * g + r
                    w = [("qh%d" % bb, hl_qk[h])] if r == 0 else []
                    if s_free[sbuf][r] is not None:
                        w.append(("sact", s_free[sbuf][r]))
                    elif hsb == 0:
                        w.append(("sdve", t_p1_dve))
                    def qk(t, r=r, kb=kb, off=off, sbuf=sbuf, bb=bb, sb=sb):
                        return t.matmul(sb_s[sbuf][:, r * 512 + off * P:(r + 1) * 512],
                                        kb_[bb][:, kb * P:(kb + 1) * P],
                                        qb_[bb][:, sb * 512 + off * P:(sb + 1) * 512],
                                        start=True, stop=True)
                    E("pe", qk, wait=w, inc="spe")
                    t_qk[r] = T("spe")
                t_sfull = t_qk[1]
                if g == npairs - 1:
                    k_used[bb] = max(k_used[bb], t_sfull)
                    if sb == 3:
                        q_used[bb] = max(q_used[bb], t_sfull)

                flush_pv()

                if g == 3 and pending_fin[0] is not None:
                    emit_finalize(pending_fin[0])
                    pending_fin[0] = None

                ptv = g % 4
                t_exp = [None, None]
                for r in range(2):
                    w = [("spe", t_qk[r])]
                    if r == 0:
                        if pt_rd[ptv] is not None:
                            w.append(("spe", pt_rd[ptv]))
                        if pt_rd_dve[ptv] is not None:
                            w.append(("sdve", pt_rd_dve[ptv]))

                    def ex(s, sbuf=sbuf, ptv=ptv, off=off, r=r):
                        sin = sb_s[sbuf][:, r * 512 + off * P:(r + 1) * 512]
                        pout = ptb[ptv][:, r * 512 + off * P:(r + 1) * 512]
                        return s.activation(pout, sin, AF.Exp, bias=zb[:], scale=1.0)
                    E("act", ex, wait=w, inc="sact")
                    t_exp[r] = T("sact")
                    s_free[sbuf][r] = t_exp[r]

                t_msk = [None, None]
                if boundary:
                    j = g
                    for r in range(2):
                        def mk(v, ptv=ptv, r=r, off=off, j=j):
                            slc = ptb[ptv][:, r * 512 + off * P:r * 512 + (off + 1) * P]
                            m3 = msk.rearrange("p (j r q) -> p j r q", j=NQB, r=2)[:, j, r, :]
                            return v.tensor_tensor(slc, slc, m3, ALU.mult)
                        w = [("sact", t_exp[r]), ("sdve", T("sdve"))]
                        if r == 0:
                            w.append(("qc0", t_p0))
                        E("dve", mk, wait=w, inc="sdve")
                        t_msk[r] = T("sdve")
                    pt_rd_dve[ptv] = t_msk[1]

                ops = []
                for r in range(2):
                    kb = 2 * g + r
                    w = [("sact", t_exp[r])]
                    if r == 0 and g == 0:
                        w.append(("qh%d" % bb, hl_v[h]))
                    if t_msk[r] is not None:
                        w.append(("sdve", t_msk[r]))
                    if r == 0 and g == 0 and a_free[ab] is not None:
                        w.append(("sdve", a_free[ab]))
                    def pv(t, r=r, kb=kb, off=off, ptv=ptv, bb=bb, g=g, a_ps=a_ps, npairs=npairs):
                        return t.matmul(a_ps[:, off * P:],
                                        vb_[bb][:, kb * P:(kb + 1) * P],
                                        ptb[ptv][:, r * 512 + off * P:(r + 1) * 512],
                                        start=(g == 0 and r == 0),
                                        stop=(g == npairs - 1 and r == 1))
                    ops.append((pv, tuple(w)))

                    def dn(t, r=r, off=off, ptv=ptv, g=g, ab=ab, npairs=npairs):
                        return t.matmul(den_row[ab][:, off * P:],
                                        ones_col[:, :],
                                        ptb[ptv][:, r * 512 + off * P:(r + 1) * 512],
                                        start=(g == 0 and r == 0),
                                        stop=(g == npairs - 1 and r == 1))
                    ops.append((dn, ()))
                pending_pv = (ops, ptv, g == npairs - 1)
            flush_pv()
            pending_fin[0] = {
                "h": h, "sb": sb, "ab": ab,
                "astg_b": astg_b, "a_ps": a_ps,
                "t_pv_done": T("spe"),
            }
            hsb += 1
        if h + 2 < H:
            emit_head_loads(h + 2)
    emit_finalize(pending_fin[0])
    pending_fin[0] = None

    t_p2_pe = T("spe")
    t_p2_ad0 = T("qad0")
    t_p2_ad1 = T("qad1")
    t_p2_dve = T("sdve")

    t_wo = [("qw", t_wo_early)]

    E("act", (lambda s: s.dma_start(
        out=wa[:, 0:8192].rearrange("p (c o) -> p c o", c=4),
        in_=wtdr[512:1024, :].rearrange("(c p) o -> p c o", p=P))),
      wait=[("spe", t_p2_pe), ("qwt", t_wt_q0)], inc="qa0", inc_by=16)
    t_wt_q1 = T("qa0")

    afm2 = [pbf[:, 0:8192], pbf[:, 8192:16384]]
    sqs = sh2k[:, 0:2048]
    y1bf2 = [pbf[:, 4096:6144], pbf[:, 6144:8192]]
    y1fm2 = [pbf[:, 10240:12288], pbf[:, 12288:14336]]
    xrl = [pf[:, 0:2048], pf[:, 2048:4096]]
    rr = [pf[:, 4096:6144], pf[:, 6144:8192]]

    afm_used = [t_p2_pe, t_p2_pe]
    xr_used = [None, None]
    wo_h0_free = [t_p2_dve, t_p2_dve]
    wo_h1_free = t_p2_dve
    r_free = [None, None]
    afm_toks_b = [None] * 4

    def p3_load_batch(b):
        bb2 = b % 2
        w = [("spe", afm_used[bb2]), ("qad0", t_p2_ad0), ("qad1", t_p2_ad1),
             ("sdve", t_p2_dve)]
        toks = []
        for (qq, c0, c1) in (("sp", 0, 8), ("pool", 8, 16)):
            def afl(s, b=b, bb2=bb2, c0=c0, c1=c1):
                return s.dma_start(
                    out=afm2[bb2].rearrange("p (c t) -> p c t", c=NIC)[:, c0:c1, :],
                    in_=adr.rearrange("(c p) t -> p c t", p=P)[:, c0:c1, b * 512:(b + 1) * 512])
            E(qq, afl, wait=w, inc=QSEM[qq], inc_by=16)
            toks.append((QSEM[qq], T(QSEM[qq])))
        afm_toks_b[b] = toks

    def emit_ln(r_t, outbuf, t_ready, extra_wait, base):
        sm = (lambda i: smf[:, base + i:base + i + 1])
        E("act", (lambda s2, r_t=r_t: s2.activation(sqs, r_t, AF.Square, accum_out=sm(3))),
          wait=[("sdve", t_ready)], inc="sact")
        t_sq = T("sact")
        E("dve", lambda v: v.tensor_add(sm(2), sm(0), sm(1)),
          wait=[("sdve", t_ready)], inc="sdve")
        E("dve", lambda v: v.tensor_scalar(sm(2), sm(2), -1.0 / D, None, ALU.mult),
          wait=[("sdve", T("sdve"))], inc="sdve")
        E("dve", lambda v: v.tensor_mul(sm(4), sm(2), sm(2)),
          wait=[("sdve", T("sdve"))], inc="sdve")
        E("dve", (lambda v: v.scalar_tensor_tensor(sm(4), sm(3), 1.0 / D, sm(4),
                                                   ALU.mult, ALU.subtract)),
          wait=[("sact", t_sq), ("sdve", T("sdve"))], inc="sdve")
        E("dve", lambda v: v.tensor_scalar(sm(4), sm(4), EPS, None, ALU.add),
          wait=[("sdve", T("sdve"))], inc="sdve")
        t_var = T("sdve")
        E("act", (lambda s2: s2.activation(sm(5), sm(4), AF.Sqrt)),
          wait=[("sdve", t_var)], inc="sact")
        t_sqr = T("sact")
        E("dve", (lambda v: v.reciprocal(sm(5), sm(5))), wait=[("sact", t_sqr)],
          inc="sdve")
        t_ln = T("sdve")
        w = [("sdve", t_ln)] + list(extra_wait)
        E("dve", (lambda v, r_t=r_t, outbuf=outbuf:
                  v.tensor_scalar(outbuf, r_t, sm(2), sm(5), ALU.add, ALU.mult)),
          wait=w, inc="sdve")
        return T("sdve")

    def p3_fin(ptb_, pt_r1):
        prsl = rr[ptb_ % 2]
        t_y1 = emit_ln(prsl, prsl, pt_r1, [], 6 * (ptb_ % 2))
        E("sp", (lambda s, ptb_=ptb_, prsl=prsl:
                 s.dma_start(out=y1dr[ptb_ * P:(ptb_ + 1) * P, :], in_=prsl)),
          wait=[("sdve", t_y1)], inc="qsp", inc_by=16)
        r_free[ptb_ % 2] = T("qsp")

    p3_load_batch(0)
    p3_load_batch(1)
    pending3 = None
    for tb in range(NQB):
        b = tb // 4
        bb2 = b % 2
        bbb = tb % 2
        rsl = rr[bbb]

        w = [("sdve", xr_used[bbb])] if xr_used[bbb] else []
        E("act", (lambda s, tb=tb, bbb=bbb:
                  s.dma_start(out=xrl[bbb], in_=xresq[tb * P:(tb + 1) * P, :])),
          wait=w, inc="qa1", inc_by=16)
        t_xr = T("qa1")

        rw = [("qsp", r_free[bbb])] if r_free[bbb] else []
        h0ps = ps_s[:, 0:1024] if bbb == 0 else ps_a[:, 0:1024]
        for half in range(2):
            psd = h0ps if half == 0 else ps_s[:, 1024:2048]
            w = list(afm_toks_b[b]) + list(t_wo)
            w.append(("sdve", wo_h0_free[bbb] if half == 0 else wo_h1_free))
            for ic in range(NIC):
                for q2 in range(2):
                    def mm(t, ic=ic, q2=q2, half=half, bb2=bb2, tb=tb, psd=psd):
                        c0 = ic * D + half * 1024 + q2 * 512
                        col = ic * 512 + (tb % 4) * P
                        if ic < 2:
                            wsrc = wo01[:, c0:c0 + 512]
                        elif ic < 4:
                            wsrc = wo23[:, c0 - 4096:c0 - 4096 + 512]
                        else:
                            wsrc = wa[:, c0:c0 + 512]
                        return t.matmul(
                            psd[:, q2 * 512:(q2 + 1) * 512],
                            afm2[bb2][:, col:col + P], wsrc,
                            start=(ic == 0), stop=(ic == NIC - 1))
                    E("pe", mm, wait=(w if ic == 0 and q2 == 0 else ()),
                      inc=("spe" if ic == NIC - 1 and q2 == 1 else None))
            t_wo_mm = T("spe")
            if half == 1 and tb % 4 == 3:
                afm_used[bb2] = t_wo_mm

            def res(v, half=half, bbb=bbb, rsl=rsl, psd=psd):
                return v.scalar_tensor_tensor(rsl[:, half * 1024:(half + 1) * 1024],
                                              psd[:, :], 1.0,
                                              xrl[bbb][:, half * 1024:(half + 1) * 1024],
                                              ALU.mult, ALU.add,
                                              accum_out=smf[:, 6 * bbb + half:6 * bbb + half + 1])
            E("dve", res, wait=[("spe", t_wo_mm), ("qa1", t_xr)] + rw, inc="sdve")
            if half == 0:
                wo_h0_free[bbb] = T("sdve")
            else:
                wo_h1_free = T("sdve")
        xr_used[bbb] = T("sdve")
        t_r1 = T("sdve")
        if tb % 4 == 3 and b + 2 < 4:
            p3_load_batch(b + 2)
        if pending3 is not None:
            p3_fin(*pending3)
        pending3 = (tb, t_r1)
    p3_fin(*pending3)
    pending3 = None

    t_p3_pe = T("spe")
    t_p3_dve = T("sdve")

    t_wt = [("qwt", t_wt_q0), ("qa0", t_wt_q1)]

    y1l = [pf[:, 0:2048], pf[:, 2048:4096]]
    y1_used = list(xr_used)
    y1fm_used = [None, None]
    y1bf_used = [None, None]
    res2_done = [t_p3_dve, t_p3_dve]
    out_st = list(r_free)
    alt_ps4 = ps_s[:, 0:512].bitcast(BF16)
    pst_free = [None, None]
    pst_eng = ["sdve", "sdve"]
    ffn_ps = [ps_a[:, 0:1024], ps_s[:, 1024:2048]]
    stage_toks = [None] * NQB
    castok = [None] * NQB

    def p4_load(tb):
        bbb = tb % 2
        w = [("sdve", y1_used[bbb])] if y1_used[bbb] else []
        E("sp", (lambda s, tb=tb, bbb=bbb:
                 s.dma_start(out=y1l[bbb], in_=y1dr[tb * P:(tb + 1) * P, :])),
          wait=w, inc="qsp", inc_by=16)
        t_yl = T("qsp")
        w = [("qsp", t_yl)]
        if tb < 2:
            w.append(("spe", t_p3_pe))
        if y1bf_used[bbb]:
            w.append(("spe", y1bf_used[bbb]))
        E("act", (lambda s2, bbb=bbb: s2.activation(y1bf2[bbb][:], y1l[bbb], AF.Copy)),
          wait=w, inc="sact")
        castok[tb] = T("sact")

    def p4_trst(tb):
        bbb = tb % 2
        y1fm = y1fm2[bbb]
        parts = []
        for gr in range(4):
            w = [("sact", castok[tb])]
            if pst_free[gr % 2]:
                w.append((pst_eng[gr % 2], pst_free[gr % 2]))
            elif tb == 0:
                w += [("spe", t_p3_pe), ("sdve", t_p3_dve)]
            if gr == 0 and y1fm_used[bbb]:
                w.append(("spe", y1fm_used[bbb]))
            tdst = ps_t if gr % 2 == 0 else alt_ps4
            for icb in range(4):
                ic = gr * 4 + icb
                def tr(t, ic=ic, icb=icb, bbb=bbb, tdst=tdst):
                    return t.transpose(tdst[:, icb * P:(icb + 1) * P],
                                       y1bf2[bbb][:, ic * P:(ic + 1) * P], idn[:])
                E("pe", tr, wait=(w if icb == 0 else ()), inc=("spe" if icb == 3 else None))
            t_tr = T("spe")
            if gr % 2 == 0:
                E("dve", (lambda v, gr=gr, y1fm=y1fm, tdst=tdst:
                          v.tensor_copy(y1fm[:, gr * 512:(gr + 1) * 512], tdst[:, 0:512])),
                  wait=[("spe", t_tr)], inc="sdve")
                pst_free[gr % 2] = T("sdve")
                pst_eng[gr % 2] = "sdve"
            else:
                E("act", (lambda s2, gr=gr, y1fm=y1fm, tdst=tdst:
                          s2.activation(y1fm[:, gr * 512:(gr + 1) * 512], tdst[:, 0:512],
                                        AF.Copy)),
                  wait=[("spe", t_tr)], inc="sact")
                pst_free[gr % 2] = T("sact")
                pst_eng[gr % 2] = "sact"
            parts.append((pst_eng[gr % 2], pst_free[gr % 2]))
        y1bf_used[bbb] = T("spe")
        stage_toks[tb] = parts

    def p4_mm(tb):
        bbb = tb % 2
        rsl = rr[bbb]
        rw = [("qsp", out_st[bbb])] if out_st[bbb] else []
        for half in range(2):
            psd = ffn_ps[half]
            w = ([(e_, v_) for (e_, v_) in stage_toks[tb]] + [("qsp", t_brow)] +
                 list(t_wt) + [("sdve", res2_done[half])])
            for ic in range(NIC):
                for q2 in range(2):
                    def mmf(t, ic=ic, q2=q2, half=half, bbb=bbb, psd=psd):
                        c0 = ic * D + half * 1024 + q2 * 512
                        if ic < 2:
                            wsrc = wt01[:, c0:c0 + 512]
                        elif ic < 4:
                            wsrc = wt23[:, c0 - 4096:c0 - 4096 + 512]
                        elif ic < 8:
                            wsrc = wa[:, c0 - 4 * D:c0 - 4 * D + 512]
                        else:
                            wsrc = wa[:, c0:c0 + 512]
                        return t.matmul(psd[:, q2 * 512:(q2 + 1) * 512],
                                        y1fm2[bbb][:, ic * P:(ic + 1) * P], wsrc,
                                        start=(ic == 0), stop=False)
                    if ic == 0 and q2 == 0:
                        wmm = w
                    elif ic == 8 and q2 == 0:
                        wmm = t_wt_late
                    else:
                        wmm = ()
                    E("pe", mmf, wait=wmm, inc=None)
            for q2 in range(2):
                def bm(t, q2=q2, half=half, psd=psd):
                    c0 = half * 1024 + q2 * 512
                    return t.matmul(psd[:, q2 * 512:(q2 + 1) * 512], ones_row_bf[:, :],
                                    brow[:, c0:c0 + 512], start=False, stop=True)
                E("pe", bm, inc=("spe" if q2 == 1 else None))
            t_ffn = T("spe")
            if half == 1:
                y1fm_used[bbb] = t_ffn

            def res2(v, half=half, bbb=bbb, rsl=rsl, psd=psd):
                return v.scalar_tensor_tensor(rsl[:, half * 1024:(half + 1) * 1024],
                                              psd[:, :], 1.0,
                                              y1l[bbb][:, half * 1024:(half + 1) * 1024],
                                              ALU.mult, ALU.add,
                                              accum_out=smf[:, half:half + 1])
            E("dve", res2, wait=[("spe", t_ffn), ("sdve", T("sdve"))] + rw, inc="sdve")
            res2_done[half] = T("sdve")
        y1_used[bbb] = T("sdve")
        return T("sdve")

    def p4_fin(tb, t_r2):
        bbb = tb % 2
        rsl = rr[bbb]
        t_out = emit_ln(rsl, rsl, t_r2, [], 0)
        E("sp", (lambda s, tb=tb, rsl=rsl: s.dma_start(out=y[tb * P:(tb + 1) * P, :], in_=rsl)),
          wait=[("sdve", t_out)], inc="qsp", inc_by=16)
        out_st[bbb] = T("qsp")

    p4_load(0)
    p4_trst(0)
    for quarter, qeng, qsem in ((2, "sp", "qsp"), (3, "act", "qa1")):
        E(qeng, (lambda s, quarter=quarter: s.dma_start(
            out=wa[:, quarter * 4 * D:(quarter + 1) * 4 * D]
            .rearrange("p (c o) -> p c o", c=4),
            in_=wtdr[quarter * 512:(quarter + 1) * 512, :]
            .rearrange("(c p) o -> p c o", p=P))),
          wait=[("spe", t_p3_pe)], inc=qsem, inc_by=16)
    t_wt_late = [("qsp", T("qsp")), ("qa1", T("qa1"))]
    p4_load(1)
    for tb in range(NQB):
        t_r2 = p4_mm(tb)
        if tb + 1 < NQB:
            p4_trst(tb + 1)
        if tb + 2 < NQB:
            p4_load(tb + 2)
        p4_fin(tb, t_r2)

    E("sp", lambda s: s.nop(), wait=[("qsp", T("qsp"))])

    from contextlib import ExitStack
    sem_names = ["qsp", "spe", "sact", "sdve", "spool", "qc0", "qw",
                 "qx0", "qx1", "qa0", "qa1", "qh0", "qh1", "qad0", "qad1", "qwt"]
    with ExitStack() as stack:
        sems = {s: stack.enter_context(nc.semaphore(s)) for s in sem_names}
        block = stack.enter_context(nc.Block())

        def replay(name, eng):
            waited = {}
            for fn, waits, inc, n in sch.streams[name]:
                for (sm, val) in waits:
                    if val > waited.get(sm, 0):
                        eng.wait_ge(sems[sm], val)
                        waited[sm] = val
                inst = fn(eng)
                if inc is not None:
                    inst.then_inc(sems[inc], n)

        @block.tensor
        def _(t):
            replay("pe", t)

        @block.scalar
        def _(s):
            replay("act", s)

        @block.vector
        def _(v):
            replay("dve", v)

        @block.gpsimd
        def _(g):
            replay("pool", g)

        @block.sync
        def _(s):
            replay("sp", s)

    return nc


def _host_inputs(inputs, core):
    b = core // 2
    parity = core % 2
    stripe = stripe_list(parity)
    x = np.asarray(inputs["L0_attn_x"])
    xb = x[b]
    rows = np.concatenate([np.arange(g * P, (g + 1) * P) for g in stripe])
    dm = np.zeros((NQB, 2, P, P), dtype=ml_dtypes.bfloat16)
    kk = np.arange(P)[:, None]
    qq = np.arange(P)[None, :]
    for j in range(NQB):
        g = stripe[j]
        for r in range(2):
            kb = 2 * j + r
            dm[j, r] = ((kb * P + kk) <= (g * P + qq)).astype(ml_dtypes.bfloat16)
    vec = np.zeros((1, 16), dtype=np.float32)
    vec[0, 0:4] = np.asarray(inputs["L0_ffn_exps"])
    vec[0, 4:8] = np.asarray(inputs["L0_ffn_bexps"])
    vec[0, 8:12] = np.asarray(inputs["L0_ffn_mask_weight"])
    vec[0, 12] = np.asarray(inputs["L0_ffn_scale"]).reshape(-1)[0]
    vec[0, 13] = np.asarray(inputs["L0_ffn_biasscale"]).reshape(-1)[0]
    im = {
        "xTkv": np.ascontiguousarray(xb.T),
        "xqT": np.ascontiguousarray(xb[rows].T),
        "xresq": np.ascontiguousarray(xb[rows]),
        "wqT": np.ascontiguousarray(np.asarray(inputs["L0_attn_Wq"]).T),
        "wkT": np.ascontiguousarray(np.asarray(inputs["L0_attn_Wk"]).T),
        "wvT": np.ascontiguousarray(np.asarray(inputs["L0_attn_Wv"]).T),
        "woT": np.ascontiguousarray(np.asarray(inputs["L0_attn_Wo"]).T),
        "pw": np.ascontiguousarray(
            np.asarray(inputs["L0_ffn_pweight"]).transpose(0, 2, 1)).reshape(D, 4 * D),
        "nw": np.ascontiguousarray(
            np.asarray(inputs["L0_ffn_nweight"]).transpose(0, 2, 1)).reshape(D, 4 * D),
        "pbias": np.ascontiguousarray(np.asarray(inputs["L0_ffn_pbias"], dtype=np.float32)),
        "nbias": np.ascontiguousarray(np.asarray(inputs["L0_ffn_nbias"], dtype=np.float32)),
        "vecs": vec,
        "dmask": np.ascontiguousarray(dm.reshape(NQB * 2 * P, P)),
    }
    return im, b, rows


def kernel(**inputs):
    nc = build_program()
    in_maps = []
    meta = []
    for c in range(8):
        im, b, rows = _host_inputs(inputs, c)
        in_maps.append(im)
        meta.append((b, rows))
    res = run_bass_kernel_spmd(nc, in_maps, core_ids=list(range(8)))
    out = np.zeros((B, S, D), dtype=np.float32)
    for c in range(8):
        b, rows = meta[c]
        out[b, rows] = res.results[c]["y"]
    return out
